# version 01
# baseline (speedup 1.0000x reference)
import os
import numpy as np
import ml_dtypes
import concourse.bass as bass
import concourse.mybir as mybir
from concourse.bass_utils import run_bass_kernel_spmd

_DEBUG = os.environ.get("BASSDBG", "") == "1"

F32 = mybir.dt.float32
BF16 = mybir.dt.bfloat16
AF = mybir.ActivationFunctionType
ALU = mybir.AluOpType
AX = mybir.AxisListType

NCORES = 8
T = 2048
D = 1024
NH = 64
DH = 64
INNER = 4096
TC = 512
NCH = T // TC
NTT = TC // 128
NKT = D // 128
NG = 32
NB = TC // 8
NTB = NB // 4
HALF = TC // 2
LN_EPS = 1e-5
ATT_EPS = 1e-6
QSCALE = 0.125


def build_nc():
    nc = bass.Bass("TRN2", target_bir_lowering=False)

    dx = nc.dram_tensor("x", [T, D], F32, kind="ExternalInput").ap()
    dwq = nc.dram_tensor("wq", [D, INNER], F32, kind="ExternalInput").ap()
    dwk = nc.dram_tensor("wk", [D, INNER], F32, kind="ExternalInput").ap()
    dwv = nc.dram_tensor("wv", [D, INNER], F32, kind="ExternalInput").ap()
    dwo = nc.dram_tensor("wo", [INNER, D], F32, kind="ExternalInput").ap()
    dwf = nc.dram_tensor("wf", [D, D], F32, kind="ExternalInput").ap()
    da1 = nc.dram_tensor("a1", [D, D], F32, kind="ExternalInput").ap()
    da2 = nc.dram_tensor("a2", [D, D], F32, kind="ExternalInput").ap()
    dbo = nc.dram_tensor("bo", [D], F32, kind="ExternalInput").ap()
    dfb = nc.dram_tensor("fb", [D], F32, kind="ExternalInput").ap()
    didb = nc.dram_tensor("idb", [128, 128], BF16, kind="ExternalInput").ap()
    didf = nc.dram_tensor("idf", [128, 128], F32, kind="ExternalInput").ap()
    dy = nc.dram_tensor("y", [T, D], F32, kind="ExternalOutput").ap()
    cq = nc.dram_tensor("cq", [NG, 128, 1024], BF16, kind="Internal").ap()
    ck = nc.dram_tensor("ck", [NG, 128, 1024], BF16, kind="Internal").ap()
    cv = nc.dram_tensor("cv", [NG, 128, 1024], BF16, kind="Internal").ap()
    cwo = nc.dram_tensor("cwo", [8, 128, 8192], BF16, kind="Internal").ap()
    cwf = nc.dram_tensor("cwf", [8, 128, 1024], BF16, kind="Internal").ap()
    ca1 = nc.dram_tensor("ca1", [8, 128, 1024], BF16, kind="Internal").ap()
    ca2 = nc.dram_tensor("ca2", [8, 128, 1024], BF16, kind="Internal").ap()
    dbg = {}
    if _DEBUG:
        dbg["xnfm"] = nc.dram_tensor("dbg_xnfm", [128, 4096], BF16,
                                     kind="ExternalOutput").ap()
        dbg["qbuf"] = nc.dram_tensor("dbg_qbuf", [128, 32768], BF16,
                                     kind="ExternalOutput").ap()
        dbg["vkbuf"] = nc.dram_tensor("dbg_vkbuf", [128, 33280], BF16,
                                      kind="ExternalOutput").ap()
        dbg["outb"] = nc.dram_tensor("dbg_outb", [128, 16384], BF16,
                                     kind="ExternalOutput").ap()
        dbg["yfm"] = nc.dram_tensor("dbg_yfm", [128, 4096], F32,
                                    kind="ExternalOutput").ap()

    QBUF = nc.alloc_sbuf_tensor("QBUF", [128, 64 * TC], BF16)
    VKBUF = nc.alloc_sbuf_tensor("VKBUF", [128, 65 * TC], BF16)
    OUTB = nc.alloc_sbuf_tensor("OUTB", [128, HALF * 64], BF16)
    XTM = nc.alloc_sbuf_tensor("XTM", [128, NTT * 1024], F32)
    XNTM = nc.alloc_sbuf_tensor("XNTM", [128, 2 * 1024], BF16)
    XNFM = nc.alloc_sbuf_tensor("XNFM", [128, NKT * TC], BF16)
    WSTRIP = nc.alloc_sbuf_tensor("WSTRIP", [128, 3 * 1024], BF16)
    STG = nc.alloc_sbuf_tensor("STG", [128, 2 * 1024], BF16)
    RECIPB = nc.alloc_sbuf_tensor("RECIPB", [128, 8], F32)
    RECTMP = nc.alloc_sbuf_tensor("RECTMP", [128, 4], F32)
    MEANS = nc.alloc_sbuf_tensor("MEANS", [128, 4], F32)
    SUMSQ = nc.alloc_sbuf_tensor("SUMSQ", [128, 4], F32)
    VARB = nc.alloc_sbuf_tensor("VARB", [128, 4], F32)
    RSTD = nc.alloc_sbuf_tensor("RSTD", [128, 4], F32)
    NT1 = nc.alloc_sbuf_tensor("NT1", [128, 4], F32)
    BOv = nc.alloc_sbuf_tensor("BOv", [128, 8], F32)
    FBv = nc.alloc_sbuf_tensor("FBv", [128, 8], F32)
    IDB = nc.alloc_sbuf_tensor("IDB", [128, 128], BF16)
    IDF = nc.alloc_sbuf_tensor("IDF", [128, 128], F32)

    obf = OUTB.ap()
    KACC = obf[:, 0:1024].bitcast(F32)
    KPAIR = obf[:, 1024:2048].bitcast(F32)
    SQD = obf[:, 2048:4096].bitcast(F32)
    qbf = QBUF.ap()
    WOSLAB = [qbf[:, 0:8192], qbf[:, 8192:16384]]
    YFM = qbf[:, 16384:24576].bitcast(F32)
    H1FM = qbf[:, 24576:28672]
    vkb = VKBUF.ap()
    H2FM = vkb[:, 0:4096]
    FSTRIP = [vkb[:, 4096 + 1024 * s: 4096 + 1024 * (s + 1)] for s in range(3)]
    H3TMP = [vkb[:, 7168:8192].bitcast(F32), vkb[:, 8192:9216].bitcast(F32)]
    YTM = [vkb[:, 9216 + 2048 * s: 9216 + 2048 * (s + 1)].bitcast(F32)
           for s in range(2)]

    Q3 = qbf[0:64, :].rearrange("p (i t) -> p i t", t=TC)
    VK3 = vkb[0:64, :].rearrange("p (i t) -> p i t", t=TC)
    OV_T = obf.rearrange("p (t i) -> p t i", i=64)

    PMM = nc.alloc_psum_tensor("PMM", [128, 1024], F32)
    PPT = nc.alloc_psum_tensor("PPT", [128, 1024], F32)
    PTR = nc.alloc_psum_tensor("PTR", [128, 2048], BF16)
    PWO = nc.alloc_psum_tensor("PWO", [128, 1024], F32)
    pmm = PMM.ap()
    ppt = PPT.ap()
    ptr = PTR.ap()
    pwo = PWO.ap()

    sem_names = ["sX", "sWg", "sWs", "sCW", "sCons", "sMM", "sEV", "sSE", "sY"]
    sems = {n: nc.alloc_semaphore(name=n) for n in sem_names}
    C = {n: 0 for n in sem_names}
    snums = sorted(s.num for s in sems.values())
    assert snums[-1] - snums[0] == len(snums) - 1
    nc.gpsimd.sem_clear(range(snums[0], snums[-1] + 1))
    nc.all_engine_barrier()

    E = {"sync": [], "gpsimd": [], "tensor": [], "vector": [], "scalar": []}

    def wait(eng, sem, tgt):
        if tgt > 0:
            E[eng].append(lambda e, s=sems[sem], t=tgt: e.wait_ge(s, t))

    def dma(eng, out, in_, sem, slow=False):
        C[sem] += 16
        tgt = C[sem]
        kw = {"allow_slow_non_contiguous": True} if slow else {}
        E[eng].append(
            lambda e, o=out, i=in_, s=sems[sem], kw=kw: e.dma_start(
                out=o, in_=i, **kw).then_inc(s, 16))
        return tgt

    def op(eng, fn, inc=None):
        if inc is None:
            E[eng].append(fn)
            return None
        C[inc] += 1
        tgt = C[inc]
        E[eng].append(lambda e, f=fn, s=sems[inc]: f(e).then_inc(s, 1))
        return tgt

    dma("sync", BOv.ap(), dbo.rearrange("(a p) -> p a", p=128), "sX", slow=True)
    dma("sync", FBv.ap(), dfb.rearrange("(a p) -> p a", p=128), "sX", slow=True)
    dma("sync", IDB.ap(), didb, "sX")
    t_pre = dma("sync", IDF.ap(), didf, "sX")

    wq3 = dwq.rearrange("(kt p) c -> p kt c", p=128)
    wk3 = dwk.rearrange("(kt p) c -> p kt c", p=128)
    wv3 = dwv.rearrange("(kt p) c -> p kt c", p=128)
    wo3 = dwo.rearrange("(i p) c -> p i c", p=64)
    wf3 = dwf.rearrange("(kt p) c -> p kt c", p=128)
    a13 = da1.rearrange("(kt p) c -> p kt c", p=128)
    a23 = da2.rearrange("(kt p) c -> p kt c", p=128)

    ev_yev = []
    last_T_event = [0]
    last_S_event = [0]

    strip_ring = [WSTRIP.ap()[:, 1024 * s: 1024 * (s + 1)] for s in range(3)]
    strip_consume_ev = {}
    gstrip = [0]

    for c in range(NCH):
        chunk0 = (c == 0)
        tok0 = TC * c

        x_tgts = []
        for tt in range(NTT):
            if c > 0:
                wait("sync", "sEV", ev_yev[c - 1][-1])
            x_tgts.append(dma("sync", XTM.ap()[:, 1024 * tt:1024 * (tt + 1)],
                              dx[tok0 + 128 * tt: tok0 + 128 * (tt + 1), :],
                              "sX"))
        for tt in range(NTT):
            xt = XTM.ap()[:, 1024 * tt:1024 * (tt + 1)]
            wait("vector", "sX", x_tgts[tt])
            if c > 0 and tt == 0:
                wait("vector", "sMM", last_T_event[0])
            op("vector", lambda e, o=MEANS.ap()[:, tt:tt + 1], i=xt:
               e.reduce_sum(o, i, axis=AX.X))
            op("vector", lambda e, i=xt: e.tensor_tensor(SQD, i, i, op=ALU.mult))
            E["vector"].append(lambda e: e.drain())
            op("vector", lambda e, a=SUMSQ.ap()[:, tt:tt + 1]:
               e.reduce_sum(a, SQD, axis=AX.X))
            E["vector"].append(lambda e: e.drain())
        ms = MEANS.ap()
        sq = SUMSQ.ap()
        vb = VARB.ap()
        rs = RSTD.ap()
        n1 = NT1.ap()

        def vdrain():
            E["vector"].append(lambda e: e.drain())

        vdrain()
        op("vector", lambda e: e.tensor_scalar(ms, ms, 1.0 / D, None, op0=ALU.mult))
        op("vector", lambda e: e.tensor_scalar(sq, sq, 1.0 / D, None, op0=ALU.mult))
        vdrain()
        op("vector", lambda e: e.tensor_tensor(vb, ms, ms, op=ALU.mult))
        vdrain()
        op("vector", lambda e: e.scalar_tensor_tensor(
            vb, vb, -1.0, sq, op0=ALU.mult, op1=ALU.add))
        vdrain()
        op("vector", lambda e: e.tensor_scalar(vb, vb, LN_EPS, None, op0=ALU.add))
        vdrain()
        op("vector", lambda e: e.reciprocal(rs, vb))
        for _ in range(6):
            vdrain()
            op("vector", lambda e: e.tensor_tensor(n1, rs, rs, op=ALU.mult))
            vdrain()
            op("vector", lambda e: e.tensor_tensor(n1, n1, vb, op=ALU.mult))
            vdrain()
            op("vector", lambda e: e.tensor_scalar(
                n1, n1, -0.5, 1.5, op0=ALU.mult, op1=ALU.add))
            vdrain()
            op("vector", lambda e: e.tensor_tensor(rs, rs, n1, op=ALU.mult))
        vdrain()
        op("vector", lambda e: e.tensor_scalar(ms, ms, -1.0, None, op0=ALU.mult))
        vdrain()
        xnfm3 = XNFM.ap().rearrange("p (kt t) -> p kt t", t=TC)
        xn_ev = []
        tev_l = []
        xnfm_ev = []
        for tt in range(NTT):
            xt = XTM.ap()[:, 1024 * tt:1024 * (tt + 1)]
            xnslot = XNTM.ap()[:, 1024 * (tt % 2):1024 * (tt % 2 + 1)]
            if tt >= 2:
                wait("vector", "sMM", tev_l[tt - 2])
            xn_ev.append(op("vector", lambda e, i=xt, o=xnslot, m=ms[:, tt:tt + 1],
                            r=rs[:, tt:tt + 1]:
                            e.tensor_scalar(o, i, m, r, op0=ALU.add, op1=ALU.mult),
                            inc="sEV"))
            bank = ptr[:, 1024 * (tt % 2):1024 * (tt % 2 + 1)]
            wait("tensor", "sX", t_pre)
            wait("tensor", "sEV", xn_ev[tt])
            if tt >= 2:
                wait("tensor", "sEV", xnfm_ev[tt - 2])
            elif c > 0:
                wait("tensor", "sEV", ev_yev[c - 1][-1])
            tev = None
            for kq in range(8):
                tev = op("tensor", lambda e, o=bank[:, 128 * kq:128 * (kq + 1)],
                         i=xnslot[:, 128 * kq:128 * (kq + 1)]:
                         e.transpose(o, i, IDB.ap()),
                         inc=("sMM" if kq == 7 else None))
            tev_l.append(tev)
            wait("vector", "sMM", tev)
            xnfm_ev.append(op("vector",
                              lambda e, i=bank.rearrange("p (kt t) -> p kt t", t=128),
                              o=xnfm3[:, :, 128 * tt:128 * (tt + 1)]:
                              e.tensor_copy(o, i), inc="sEV"))
        wait("tensor", "sEV", xnfm_ev[-1])
        if _DEBUG and chunk0:
            wait("sync", "sEV", xnfm_ev[-1])
            dma("sync", dbg["xnfm"], XNFM.ap(), "sY")

        def stream_phase(n_strips, cache_ap, f32_loader, evac_fn, first_gate=None):
            evac_tgts = {}
            for g in range(n_strips):
                gi = gstrip[0]
                gstrip[0] += 1
                slot = strip_ring[gi % 3]
                if gi - 3 in strip_consume_ev:
                    wait("gpsimd" if chunk0 else "sync", "sMM",
                         strip_consume_ev[gi - 3])
                if chunk0:
                    arr = f32_loader(g, slot)
                    wait("gpsimd", "sWg", arr[1])
                    dma("gpsimd", cache_ap[g], slot, "sCW")
                else:
                    if g == 0 and first_gate is not None:
                        wait("sync", first_gate[0], first_gate[1])
                    arr = ("sWs", dma("sync", slot, cache_ap[g], "sWs"))
                half = pmm[:, 512 * (g % 2):512 * (g % 2 + 1)]
                wait("tensor", arr[0], arr[1])
                if g >= 2:
                    sname, stgt = evac_tgts[g - 2]
                    wait("tensor", sname, stgt)
                mmev = None
                for kt in range(NKT):
                    mmev = op("tensor",
                              lambda e, o=half, l=slot.rearrange(
                                  "p (kt c) -> p kt c", c=128)[:, kt, :],
                              r=XNFM.ap()[:, TC * kt:TC * (kt + 1)],
                              st=(kt == 0), sp=(kt == NKT - 1):
                              e.matmul(o, l, r, start=st, stop=sp),
                              inc=("sMM" if kt == NKT - 1 else None))
                strip_consume_ev[gi] = mmev
                evac_tgts[g] = evac_fn(g, half, mmev)
            return evac_tgts

        def k_loader(g, slot):
            s3 = slot.rearrange("p (kt c) -> p kt c", c=128)
            dma("gpsimd", s3[:, :, 0:64], wk3[:, :, 64 * g:64 * (g + 1)], "sWg")
            return ("sWg", dma("gpsimd", s3[:, :, 64:128],
                               wk3[:, :, 2048 + 64 * g:2048 + 64 * (g + 1)],
                               "sWg"))

        def k_evac(g, half, mmev):
            wait("vector", "sMM", mmev)
            if g == 0 and c > 0:
                wait("vector", "sMM", last_T_event[0])
            if g == 0:
                t = op("vector", lambda e, i=half:
                       e.tensor_scalar(KACC, i, 0.0, None, op0=ALU.max),
                       inc="sEV")
            else:
                t = op("vector", lambda e, i=half:
                       e.scalar_tensor_tensor(KACC, i, 0.0, KACC,
                                              op0=ALU.max, op1=ALU.add),
                       inc="sEV")
            return ("sEV", t)

        k_ev = stream_phase(NG, ck, k_loader, k_evac)
        wait("scalar", "sEV", k_ev[NG - 1][1])
        t1 = dma("scalar", KPAIR[0:64, :], KACC[64:128, :], "sCons")
        wait("vector", "sCons", t1)
        fold_ev = op("vector", lambda e:
                     e.tensor_tensor(KPAIR[0:64, :], KACC[0:64, :],
                                     KPAIR[0:64, :], op=ALU.add), inc="sEV")
        aug_ev = op("vector", lambda e:
                    e.tensor_copy(vkb[0:64, 64 * TC:65 * TC], KPAIR[0:64, :]),
                    inc="sEV")
        wait("scalar", "sEV", aug_ev)
        kp_dup = dma("scalar", KPAIR[64:128, :], KPAIR[0:64, :], "sCons")

        def q_loader(g, slot):
            s3 = slot.rearrange("p (kt c) -> p kt c", c=128)
            dma("gpsimd", s3[:, :, 0:64], wq3[:, :, 64 * g:64 * (g + 1)], "sWg")
            return ("sWg", dma("gpsimd", s3[:, :, 64:128],
                               wq3[:, :, 2048 + 64 * g:2048 + 64 * (g + 1)],
                               "sWg"))

        def q_evac(g, half, mmev):
            wait("scalar", "sMM", mmev)
            if g == 0 and c > 0:
                wait("scalar", "sMM", last_T_event[0])
            t = op("scalar", lambda e, i=half, o=qbf[:, TC * g:TC * (g + 1)]:
                   e.activation(o, i, AF.Relu, scale=QSCALE), inc="sSE")
            return ("sSE", t)

        q_ev = stream_phase(NG, cq, q_loader, q_evac)
        wait("scalar", "sSE", q_ev[NG - 1][1])
        q_cons = dma("scalar", qbf[0:64, 32 * TC:64 * TC], qbf[64:128, 0:32 * TC],
                     "sCons")

        def v_loader(g, slot):
            s3 = slot.rearrange("p (kt c) -> p kt c", c=128)
            dma("gpsimd", s3[:, :, 0:64], wv3[:, :, 64 * g:64 * (g + 1)], "sWg")
            return ("sWg", dma("gpsimd", s3[:, :, 64:128],
                               wv3[:, :, 2048 + 64 * g:2048 + 64 * (g + 1)],
                               "sWg"))

        def v_evac(g, half, mmev):
            wait("vector", "sMM", mmev)
            if g == 0:
                wait("vector", "sCons", kp_dup)
                if c > 0:
                    wait("vector", "sY", 16 * 4 * c)
            t = op("vector", lambda e, i=half, o=vkb[:, TC * g:TC * (g + 1)]:
                   e.tensor_tensor(o, i, KPAIR, op=ALU.mult), inc="sEV")
            return ("sEV", t)

        v_ev = stream_phase(NG, cv, v_loader, v_evac)
        wait("scalar", "sEV", v_ev[NG - 1][1])
        vk_cons = dma("scalar", vkb[0:64, 32 * TC:64 * TC], vkb[64:128, 0:32 * TC],
                      "sCons")

        if _DEBUG and chunk0:
            wait("sync", "sCons", q_cons)
            wait("sync", "sCons", vk_cons)
            dma("sync", dbg["qbuf"], qbf, "sY")
            dma("sync", dbg["vkbuf"], vkb, "sY")

        wait("tensor", "sCons", q_cons)
        wait("tensor", "sCons", vk_cons)
        if c > 0:
            wait("tensor", "sSE", last_S_event[0])
        e_mm = [None] * NB
        e_div = [None] * NB
        e_tb = [None] * NTB
        e_ev2_v = [None] * NTB
        e_ev2_s = [None] * NTB

        def emit_pt_mms(b):
            par = b % 2
            base = ppt[:, 512 * par: 512 * par + 260]
            if b >= 2:
                wait("tensor", "sEV", e_div[b - 2])
            ev = None
            for s4 in range(4):
                pr = 2 * b + (s4 // 2)
                hoff = HALF * (s4 % 2)
                tA = 2 * pr + hoff
                tB = tA + 1
                for u, tk in enumerate((tA, tB)):
                    o = base[64 * u:64 * (u + 1), 65 * s4: 65 * (s4 + 1)]
                    l = Q3[:, :, tk]
                    r = VK3[:, :, tk]
                    last = (s4 == 3 and u == 1)
                    ev = op("tensor",
                            lambda e, o=o, l=l, r=r, tp=(0, 64 * u):
                            e.matmul(o, l, r, start=True, stop=True,
                                     tile_position=tp, skip_group_check=True),
                            inc=("sMM" if last else None))
            e_mm[b] = ev

        def emit_pt_round(b):
            par = b % 2
            b3 = ppt[:, 512 * par: 512 * par + 260]\
                .rearrange("p (s j) -> p s j", j=65)
            wait("vector", "sMM", e_mm[b])
            op("vector", lambda e, i=b3[:, :, 64]:
               e.tensor_scalar(RECTMP.ap(), i, ATT_EPS, None, op0=ALU.add))
            if b >= 1:
                emit_div(b - 1)
            else:
                E["vector"].append(lambda e: e.drain())
            rslot = RECIPB.ap()[:, 4 * (b % 2):4 * (b % 2 + 1)]
            op("vector", lambda e, o=rslot: e.reciprocal(o, RECTMP.ap()))
            E["vector"].append(lambda e: e.drain())

        def emit_div(b):
            par = b % 2
            b3 = ppt[:, 512 * par: 512 * par + 260]\
                .rearrange("p (s j) -> p s j", j=65)
            R = b // 4
            if R >= 2:
                wait("vector", "sMM", e_tb[R - 2])
            rslot = RECIPB.ap()[:, 4 * (b % 2):4 * (b % 2 + 1)]
            stg = STG.ap()[:, 1024 * (R % 2) + 256 * (b % 4):
                           1024 * (R % 2) + 256 * (b % 4 + 1)]
            e_div[b] = op("vector",
                          lambda e, o=stg.rearrange("p (s j) -> p s j", j=64),
                          i=b3[:, :, 0:64],
                          r=rslot.unsqueeze(2).broadcast_to([128, 4, 64]):
                          e.tensor_tensor(o, i, r, op=ALU.mult), inc="sEV")

        def emit_pt_transposes(b):
            R = b // 4
            bank = ptr[:, 1024 * (R % 2):1024 * (R % 2 + 1)]
            wait("tensor", "sEV", e_div[b])
            if b % 4 == 0:
                if R >= 2:
                    wait("tensor", "sEV", e_ev2_v[R - 2])
                    wait("tensor", "sSE", e_ev2_s[R - 2])
                elif c > 0 and R == 0:
                    pass
            stg = STG.ap()[:, 1024 * (R % 2) + 256 * (b % 4):
                           1024 * (R % 2) + 256 * (b % 4 + 1)]
            ev = None
            for m in range(2):
                last = (b % 4 == 3 and m == 1)
                ev = op("tensor",
                        lambda e, o=bank[:, 256 * (b % 4) + 128 * m:
                                         256 * (b % 4) + 128 * (m + 1)],
                        i=stg[:, 128 * m:128 * (m + 1)]:
                        e.transpose(o, i, IDB.ap()),
                        inc=("sMM" if last else None))
            if b % 4 == 3:
                e_tb[R] = ev

        def emit_ev2(R):
            bank = ptr[:, 1024 * (R % 2):1024 * (R % 2 + 1)]
            wait("scalar", "sMM", e_tb[R])
            if c > 0 and R == 0:
                wait("scalar", "sMM", last_T_event[0])
            e_ev2_s[R] = op("scalar",
                            lambda e, o=obf[0:64, 1024 * R:1024 * (R + 1)],
                            i=bank[0:64, :]: e.copy(o, i), inc="sSE")
            wait("vector", "sMM", e_tb[R])
            e_ev2_v[R] = op("vector",
                            lambda e, o=obf[64:128, 1024 * R:1024 * (R + 1)],
                            i=bank[64:128, :]: e.tensor_copy(o, i), inc="sEV")

        for b in range(NB):
            emit_pt_mms(b)
            emit_pt_round(b)
            if b >= 1:
                emit_pt_transposes(b - 1)
            if b >= 1 and (b - 1) % 4 == 3:
                emit_ev2((b - 1) // 4)
        emit_div(NB - 1)
        emit_pt_transposes(NB - 1)
        emit_ev2(NTB - 1)
        pt_end_T = e_tb[NTB - 1]

        e_wo = [None] * 8
        e_woev = [None] * 8
        slab_arr = [None] * 8

        def emit_slab_dma(ct):
            slab = WOSLAB[ct % 2]
            dq = "gpsimd" if chunk0 else "sync"
            wait(dq, "sMM", pt_end_T if ct < 2 else e_wo[ct - 2])
            if chunk0:
                t_cast = dma("gpsimd",
                             slab[0:64, :].rearrange("p (i c) -> p i c", c=128),
                             wo3[:, :, 128 * ct:128 * (ct + 1)], "sWg")
                wait("gpsimd", "sWg", t_cast)
                t_dup = dma("gpsimd", slab[64:128, :], slab[0:64, :], "sWg")
                slab_arr[ct] = ("sWg", t_dup)
                wait("gpsimd", "sWg", t_dup)
                dma("gpsimd", cwo[ct], slab, "sCW")
            else:
                slab_arr[ct] = ("sWs", dma("sync", slab, cwo[ct], "sWs"))

        emit_slab_dma(0)
        emit_slab_dma(1)
        if _DEBUG and chunk0:
            wait("sync", "sSE", e_ev2_s[NTB - 1])
            wait("sync", "sEV", e_ev2_v[NTB - 1])
            dma("sync", dbg["outb"], obf, "sY")
        wait("tensor", "sSE", e_ev2_s[NTB - 1])
        wait("tensor", "sEV", e_ev2_v[NTB - 1])
        for ct in range(8):
            slab = WOSLAB[ct % 2]
            s3 = slab.rearrange("p (i c) -> p i c", c=128)
            wait("tensor", slab_arr[ct][0], slab_arr[ct][1])
            if ct >= 2:
                wait("tensor", "sSE", e_woev[ct - 2])
            bank_h1 = pwo[:, 0:256] if ct % 2 == 0 else ppt[:, 0:256]
            bank_h2 = pwo[:, 512:768] if ct % 2 == 0 else ppt[:, 512:768]
            ev = None
            for i in range(64):
                st = (i == 0)
                sp = (i == 63)
                op("tensor", lambda e, o=bank_h1, l=s3[0:64, i, :],
                   r=OV_T[0:64, :, i], st=st, sp=sp:
                   e.matmul(o, l, r, start=st, stop=sp, skip_group_check=True))
                ev = op("tensor", lambda e, o=bank_h2, l=s3[64:128, i, :],
                        r=OV_T[64:128, :, i], st=st, sp=sp:
                        e.matmul(o, l, r, start=st, stop=sp,
                                 skip_group_check=True),
                        inc=("sMM" if sp else None))
            e_wo[ct] = ev
            wait("scalar", "sMM", ev)
            op("scalar", lambda e, i=bank_h1, o=YFM[:, 512 * ct:512 * ct + 256],
               b=BOv.ap()[:, ct:ct + 1]:
               e.activation(o, i, AF.Identity, bias=b))
            e_woev[ct] = op("scalar",
                            lambda e, i=bank_h2,
                            o=YFM[:, 512 * ct + 256:512 * ct + 512],
                            b=BOv.ap()[:, ct:ct + 1]:
                            e.activation(o, i, AF.Identity, bias=b), inc="sSE")
            if ct + 2 < 8:
                emit_slab_dma(ct + 2)

        ffn_in = [XNFM.ap(), H1FM, H2FM]
        ffn_cache = [cwf, ca1, ca2]
        ffn_f32 = [wf3, a13, a23]
        e_ffn = [[None] * 8 for _ in range(3)]
        e_fev = [[None] * 8 for _ in range(3)]
        tanh_ev = [None] * 8
        add_ev = [None] * 8
        fstrip_consumed = {}
        fidx = [0]
        for s in range(3):
            for g in range(8):
                fi = fidx[0]
                fidx[0] += 1
                slot = FSTRIP[fi % 3]
                dq = "gpsimd" if chunk0 else "sync"
                if fi < 3:
                    wait(dq, "sMM", pt_end_T)
                else:
                    wait(dq, "sMM", fstrip_consumed[fi - 3])
                if chunk0:
                    dma("gpsimd", slot.rearrange("p (kt c) -> p kt c", c=128),
                        ffn_f32[s][:, :, 128 * g:128 * (g + 1)], "sWg")
                    arr = ("sWg", C["sWg"])
                    wait("gpsimd", "sWg", arr[1])
                    dma("gpsimd", ffn_cache[s][g], slot, "sCW")
                else:
                    arr = ("sWs", dma("sync", slot, ffn_cache[s][g], "sWs"))
                half = pmm[:, 512 * (g % 2):512 * (g % 2 + 1)]
                wait("tensor", arr[0], arr[1])
                if s == 0 and g < 2:
                    pass
                if g >= 2 or s > 0:
                    pg = (s, g - 2) if g >= 2 else (s - 1, g + 6)
                    sname = "sSE" if pg[0] == 2 else "sEV"
                    tgtl = e_fev[pg[0]][pg[1]]
                    wait("tensor", sname, tgtl)
                if s > 0 and g == 0:
                    wait("tensor", "sEV", e_fev[s - 1][7])
                mmev = None
                for kt in range(NKT):
                    mmev = op("tensor",
                              lambda e, o=half,
                              l=slot.rearrange("p (kt c) -> p kt c", c=128)[:, kt, :],
                              r=ffn_in[s][:, TC * kt:TC * (kt + 1)],
                              st=(kt == 0), sp=(kt == NKT - 1):
                              e.matmul(o, l, r, start=st, stop=sp),
                              inc=("sMM" if kt == NKT - 1 else None))
                fstrip_consumed[fi] = mmev
                e_ffn[s][g] = mmev
                if s < 2:
                    dst = (H1FM if s == 0 else H2FM)
                    wait("vector", "sMM", mmev)
                    e_fev[s][g] = op("vector",
                                     lambda e, i=half, o=dst[:, TC * g:TC * (g + 1)]:
                                     e.tensor_copy(o, i), inc="sEV")
                else:
                    wait("scalar", "sMM", mmev)
                    if g >= 2:
                        wait("scalar", "sEV", add_ev[g - 2])
                    h3 = H3TMP[g % 2]
                    tanh_ev[g] = op("scalar",
                                    lambda e, i=half, o=h3, b=FBv.ap()[:, g:g + 1]:
                                    e.activation(o, i, AF.Tanh, bias=b), inc="sSE")
                    e_fev[s][g] = tanh_ev[g]
                    wait("vector", "sSE", tanh_ev[g])
                    add_ev[g] = op("vector",
                                   lambda e, a=YFM[:, TC * g:TC * (g + 1)], h=h3:
                                   e.tensor_tensor(a, a, h, op=ALU.add), inc="sEV")

        if _DEBUG and chunk0:
            wait("sync", "sEV", add_ev[7])
            dma("sync", dbg["yfm"], YFM, "sY")

        yev_list = []
        y_dma_tgts = {}
        ev_prev_tile = None
        for tt in range(NTT):
            ytm = YTM[tt % 2]
            yevs = []
            for hb in range(2):
                bank = pmm[:, 512 * hb:512 * (hb + 1)]
                if tt == 0 and hb == 0:
                    wait("tensor", "sSE", tanh_ev[7])
                    wait("tensor", "sEV", add_ev[7])
                if tt >= 1:
                    wait("tensor", "sEV", ev_prev_tile)
                tev = None
                for kq in range(4):
                    kt = 4 * hb + kq
                    tev = op("tensor",
                             lambda e, o=bank[:, 128 * kq:128 * (kq + 1)],
                             i=YFM[:, TC * kt + 128 * tt:TC * kt + 128 * (tt + 1)]:
                             e.transpose(o, i, IDF.ap()),
                             inc=("sMM" if kq == 3 else None))
                wait("vector", "sMM", tev)
                if tt >= 2:
                    wait("vector", "sY", y_dma_tgts[tt - 2])
                yev = op("vector",
                         lambda e, o=ytm[:, 512 * hb:512 * (hb + 1)], i=bank,
                         x=XTM.ap()[:, 1024 * tt + 512 * hb:1024 * tt + 512 * (hb + 1)]:
                         e.scalar_tensor_tensor(o, i, 0.0, x,
                                                op0=ALU.bypass, op1=ALU.add),
                         inc="sEV")
                yevs.append(yev)
            ev_prev_tile = yevs[-1]
            yev_list.extend(yevs)
            wait("sync", "sEV", yevs[-1])
            y_dma_tgts[tt] = dma(
                "sync", dy[tok0 + 128 * tt: tok0 + 128 * (tt + 1), :], ytm, "sY")
        ev_yev.append(yev_list)
        last_T_event[0] = C["sMM"]
        last_S_event[0] = C["sSE"]
        if chunk0:
            cache_done = C["sCW"]
            wait("sync", "sCW", cache_done)

    wait("sync", "sY", C["sY"])

    with nc.Block() as block:
        @block.sync
        def _(eng):
            for f in E["sync"]:
                f(eng)

        @block.gpsimd
        def _(eng):
            for f in E["gpsimd"]:
                f(eng)

        @block.tensor
        def _(eng):
            for f in E["tensor"]:
                f(eng)

        @block.vector
        def _(eng):
            for f in E["vector"]:
                f(eng)

        @block.scalar
        def _(eng):
            for f in E["scalar"]:
                f(eng)

    return nc


_NC_CACHE = None


def kernel(L0_attn_x, L0_attn_Wq, L0_attn_Wk, L0_attn_Wv, L0_attn_Wo,
           L0_attn_bo, L0_ffn_weight, L0_ffn_aux_weight1, L0_ffn_aux_weight2,
           L0_ffn_bias):
    global _NC_CACHE
    x = np.ascontiguousarray(np.asarray(L0_attn_x, dtype=np.float32))
    wq = np.ascontiguousarray(np.asarray(L0_attn_Wq, dtype=np.float32))
    wk = np.ascontiguousarray(np.asarray(L0_attn_Wk, dtype=np.float32))
    wv = np.ascontiguousarray(np.asarray(L0_attn_Wv, dtype=np.float32))
    wo = np.ascontiguousarray(np.asarray(L0_attn_Wo, dtype=np.float32))
    bo = np.ascontiguousarray(np.asarray(L0_attn_bo, dtype=np.float32))
    wf = np.ascontiguousarray(np.asarray(L0_ffn_weight, dtype=np.float32))
    a1 = np.ascontiguousarray(np.asarray(L0_ffn_aux_weight1, dtype=np.float32))
    a2 = np.ascontiguousarray(np.asarray(L0_ffn_aux_weight2, dtype=np.float32))
    fb = np.ascontiguousarray(np.asarray(L0_ffn_bias, dtype=np.float32))
    idb = np.eye(128, dtype=np.float32).astype(ml_dtypes.bfloat16)
    idf = np.eye(128, dtype=np.float32)

    if _NC_CACHE is None:
        _NC_CACHE = build_nc()
    nc = _NC_CACHE
    common = {"wq": wq, "wk": wk, "wv": wv, "wo": wo, "bo": bo,
              "wf": wf, "a1": a1, "a2": a2, "fb": fb, "idb": idb, "idf": idf}
    in_maps = [dict(common, x=x[c]) for c in range(NCORES)]
    res = run_bass_kernel_spmd(nc, in_maps, core_ids=list(range(NCORES)))
    if _DEBUG:
        for k in ["dbg_xnfm", "dbg_qbuf", "dbg_vkbuf", "dbg_outb", "dbg_yfm"]:
            np.save("/tmp/%s.npy" % k,
                    np.asarray(res.results[0][k]).astype(np.float32))
    y = np.stack([np.asarray(res.results[c]["y"]) for c in range(NCORES)])
    return y


# version 03
# speedup vs baseline: 1.1257x; 1.1257x over previous
import os
import numpy as np
import ml_dtypes
import concourse.bass as bass
import concourse.mybir as mybir
from concourse.bass_utils import run_bass_kernel_spmd

_DEBUG = os.environ.get("BASSDBG", "") == "1"

F32 = mybir.dt.float32
BF16 = mybir.dt.bfloat16
AF = mybir.ActivationFunctionType
ALU = mybir.AluOpType
AX = mybir.AxisListType

NCORES = 8
T = 2048
D = 1024
NH = 64
DH = 64
INNER = 4096
TC = 512
NCH = T // TC
NTT = TC // 128
NKT = D // 128
NG = 32
NB = TC // 8
NTB = NB // 4
HALF = TC // 2
LN_EPS = 1e-5
ATT_EPS = 1e-6
QSCALE = 0.125


def build_nc():
    nc = bass.Bass("TRN2", target_bir_lowering=False)

    dx = nc.dram_tensor("x", [T, D], F32, kind="ExternalInput").ap()
    dwq = nc.dram_tensor("wq", [D, INNER], F32, kind="ExternalInput").ap()
    dwk = nc.dram_tensor("wk", [D, INNER], F32, kind="ExternalInput").ap()
    dwv = nc.dram_tensor("wv", [D, INNER], F32, kind="ExternalInput").ap()
    dwo = nc.dram_tensor("wo", [INNER, D], F32, kind="ExternalInput").ap()
    dwf = nc.dram_tensor("wf", [D, D], F32, kind="ExternalInput").ap()
    da1 = nc.dram_tensor("a1", [D, D], F32, kind="ExternalInput").ap()
    da2 = nc.dram_tensor("a2", [D, D], F32, kind="ExternalInput").ap()
    dbo = nc.dram_tensor("bo", [D], F32, kind="ExternalInput").ap()
    dfb = nc.dram_tensor("fb", [D], F32, kind="ExternalInput").ap()
    didb = nc.dram_tensor("idb", [128, 128], BF16, kind="ExternalInput").ap()
    didf = nc.dram_tensor("idf", [128, 128], F32, kind="ExternalInput").ap()
    dy = nc.dram_tensor("y", [T, D], F32, kind="ExternalOutput").ap()
    cq = nc.dram_tensor("cq", [NG, 128, 1024], BF16, kind="Internal").ap()
    ck = nc.dram_tensor("ck", [NG, 128, 1024], BF16, kind="Internal").ap()
    cv = nc.dram_tensor("cv", [NG, 128, 1024], BF16, kind="Internal").ap()
    cwo = nc.dram_tensor("cwo", [8, 128, 8192], BF16, kind="Internal").ap()
    cwf = nc.dram_tensor("cwf", [8, 128, 1024], BF16, kind="Internal").ap()
    ca1 = nc.dram_tensor("ca1", [8, 128, 1024], BF16, kind="Internal").ap()
    ca2 = nc.dram_tensor("ca2", [8, 128, 1024], BF16, kind="Internal").ap()
    dbg = {}
    if _DEBUG:
        dbg["xnfm"] = nc.dram_tensor("dbg_xnfm", [128, 4096], BF16,
                                     kind="ExternalOutput").ap()
        dbg["qbuf"] = nc.dram_tensor("dbg_qbuf", [128, 32768], BF16,
                                     kind="ExternalOutput").ap()
        dbg["vkbuf"] = nc.dram_tensor("dbg_vkbuf", [128, 33280], BF16,
                                      kind="ExternalOutput").ap()
        dbg["outb"] = nc.dram_tensor("dbg_outb", [128, 16384], BF16,
                                     kind="ExternalOutput").ap()
        dbg["yfm"] = nc.dram_tensor("dbg_yfm", [128, 4096], F32,
                                    kind="ExternalOutput").ap()

    QBUF = nc.alloc_sbuf_tensor("QBUF", [128, 64 * TC], BF16)
    VKBUF = nc.alloc_sbuf_tensor("VKBUF", [128, 65 * TC], BF16)
    OUTB = nc.alloc_sbuf_tensor("OUTB", [128, HALF * 64], BF16)
    XTM = nc.alloc_sbuf_tensor("XTM", [128, NTT * 1024], F32)
    XNTM = nc.alloc_sbuf_tensor("XNTM", [128, 2 * 1024], BF16)
    XNFM = nc.alloc_sbuf_tensor("XNFM", [128, NKT * TC], BF16)
    WSTRIP = nc.alloc_sbuf_tensor("WSTRIP", [128, 3 * 1024], BF16)
    STG = nc.alloc_sbuf_tensor("STG", [128, 2 * 1024], BF16)
    RECIPB = nc.alloc_sbuf_tensor("RECIPB", [128, 8], F32)
    RECTMP = nc.alloc_sbuf_tensor("RECTMP", [128, 4], F32)
    MEANS = nc.alloc_sbuf_tensor("MEANS", [128, 4], F32)
    SUMSQ = nc.alloc_sbuf_tensor("SUMSQ", [128, 4], F32)
    VARB = nc.alloc_sbuf_tensor("VARB", [128, 4], F32)
    RSTD = nc.alloc_sbuf_tensor("RSTD", [128, 4], F32)
    NT1 = nc.alloc_sbuf_tensor("NT1", [128, 4], F32)
    BOv = nc.alloc_sbuf_tensor("BOv", [128, 8], F32)
    FBv = nc.alloc_sbuf_tensor("FBv", [128, 8], F32)
    IDB = nc.alloc_sbuf_tensor("IDB", [128, 128], BF16)
    IDF = nc.alloc_sbuf_tensor("IDF", [128, 128], F32)

    obf = OUTB.ap()
    KACC = obf[:, 0:1024].bitcast(F32)
    KPAIR = obf[:, 1024:2048].bitcast(F32)
    SQD = obf[:, 2048:4096].bitcast(F32)
    qbf = QBUF.ap()
    WOSLAB = [qbf[:, 0:8192], qbf[:, 8192:16384]]
    YFM = qbf[:, 16384:24576].bitcast(F32)
    H1FM = qbf[:, 24576:28672]
    vkb = VKBUF.ap()
    H2FM = vkb[:, 0:4096]
    FSTRIP = [vkb[:, 4096 + 1024 * s: 4096 + 1024 * (s + 1)] for s in range(3)]
    H3TMP = [vkb[:, 7168:8192].bitcast(F32), vkb[:, 8192:9216].bitcast(F32)]
    YTM = [vkb[:, 9216 + 2048 * s: 9216 + 2048 * (s + 1)].bitcast(F32)
           for s in range(2)]

    Q3 = qbf[0:64, :].rearrange("p (i t) -> p i t", t=TC)
    VK3 = vkb[0:64, :].rearrange("p (i t) -> p i t", t=TC)
    OV_I = [obf[0:64, :].rearrange("p (i t) -> p i t", t=HALF),
            obf[64:128, :].rearrange("p (i t) -> p i t", t=HALF)]

    PMM = nc.alloc_psum_tensor("PMM", [128, 1024], F32)
    PPT = nc.alloc_psum_tensor("PPT", [128, 1024], F32)
    PTR = nc.alloc_psum_tensor("PTR", [128, 2048], BF16)
    PWO = nc.alloc_psum_tensor("PWO", [128, 1024], F32)
    pmm = PMM.ap()
    ppt = PPT.ap()
    ptr = PTR.ap()
    pwo = PWO.ap()

    sem_names = ["sX", "sWg", "sWs", "sCW", "sCons", "sMM", "sEV", "sSE", "sY"]
    sems = {n: nc.alloc_semaphore(name=n) for n in sem_names}
    C = {n: 0 for n in sem_names}
    snums = sorted(s.num for s in sems.values())
    assert snums[-1] - snums[0] == len(snums) - 1
    nc.gpsimd.sem_clear(range(snums[0], snums[-1] + 1))
    nc.all_engine_barrier()

    E = {"sync": [], "gpsimd": [], "tensor": [], "vector": [], "scalar": []}

    def wait(eng, sem, tgt):
        if tgt > 0:
            E[eng].append(lambda e, s=sems[sem], t=tgt: e.wait_ge(s, t))

    def dma(eng, out, in_, sem, slow=False):
        C[sem] += 16
        tgt = C[sem]
        kw = {"allow_slow_non_contiguous": True} if slow else {}
        E[eng].append(
            lambda e, o=out, i=in_, s=sems[sem], kw=kw: e.dma_start(
                out=o, in_=i, **kw).then_inc(s, 16))
        return tgt

    def op(eng, fn, inc=None):
        if inc is None:
            E[eng].append(fn)
            return None
        C[inc] += 1
        tgt = C[inc]
        E[eng].append(lambda e, f=fn, s=sems[inc]: f(e).then_inc(s, 1))
        return tgt

    dma("sync", BOv.ap(), dbo.rearrange("(a p) -> p a", p=128), "sX", slow=True)
    dma("sync", FBv.ap(), dfb.rearrange("(a p) -> p a", p=128), "sX", slow=True)
    dma("sync", IDB.ap(), didb, "sX")
    t_pre = dma("sync", IDF.ap(), didf, "sX")

    wq3 = dwq.rearrange("(kt p) c -> p kt c", p=128)
    wk3 = dwk.rearrange("(kt p) c -> p kt c", p=128)
    wv3 = dwv.rearrange("(kt p) c -> p kt c", p=128)
    wo3 = dwo.rearrange("(i p) c -> p i c", p=64)
    wf3 = dwf.rearrange("(kt p) c -> p kt c", p=128)
    a13 = da1.rearrange("(kt p) c -> p kt c", p=128)
    a23 = da2.rearrange("(kt p) c -> p kt c", p=128)

    ev_yev = []
    last_T_event = [0]
    last_S_event = [0]

    strip_ring = [WSTRIP.ap()[:, 1024 * s: 1024 * (s + 1)] for s in range(3)]
    strip_consume_ev = {}
    gstrip = [0]

    for c in range(NCH):
        chunk0 = (c == 0)
        tok0 = TC * c

        x_tgts = []
        for tt in range(NTT):
            if c > 0:
                wait("sync", "sEV", ev_yev[c - 1][-1])
            x_tgts.append(dma("sync", XTM.ap()[:, 1024 * tt:1024 * (tt + 1)],
                              dx[tok0 + 128 * tt: tok0 + 128 * (tt + 1), :],
                              "sX"))
        for tt in range(NTT):
            xt = XTM.ap()[:, 1024 * tt:1024 * (tt + 1)]
            wait("vector", "sX", x_tgts[tt])
            if c > 0 and tt == 0:
                wait("vector", "sMM", last_T_event[0])
            op("vector", lambda e, o=MEANS.ap()[:, tt:tt + 1], i=xt:
               e.reduce_sum(o, i, axis=AX.X))
            op("vector", lambda e, i=xt: e.tensor_tensor(SQD, i, i, op=ALU.mult))
            E["vector"].append(lambda e: e.drain())
            op("vector", lambda e, a=SUMSQ.ap()[:, tt:tt + 1]:
               e.reduce_sum(a, SQD, axis=AX.X))
            E["vector"].append(lambda e: e.drain())
        ms = MEANS.ap()
        sq = SUMSQ.ap()
        vb = VARB.ap()
        rs = RSTD.ap()
        n1 = NT1.ap()

        def vdrain():
            E["vector"].append(lambda e: e.drain())

        vdrain()
        op("vector", lambda e: e.tensor_scalar(ms, ms, 1.0 / D, None, op0=ALU.mult))
        op("vector", lambda e: e.tensor_scalar(sq, sq, 1.0 / D, None, op0=ALU.mult))
        vdrain()
        op("vector", lambda e: e.tensor_tensor(vb, ms, ms, op=ALU.mult))
        vdrain()
        op("vector", lambda e: e.scalar_tensor_tensor(
            vb, vb, -1.0, sq, op0=ALU.mult, op1=ALU.add))
        vdrain()
        op("vector", lambda e: e.tensor_scalar(vb, vb, LN_EPS, None, op0=ALU.add))
        vdrain()
        op("vector", lambda e: e.reciprocal(rs, vb))
        for _ in range(6):
            vdrain()
            op("vector", lambda e: e.tensor_tensor(n1, rs, rs, op=ALU.mult))
            vdrain()
            op("vector", lambda e: e.tensor_tensor(n1, n1, vb, op=ALU.mult))
            vdrain()
            op("vector", lambda e: e.tensor_scalar(
                n1, n1, -0.5, 1.5, op0=ALU.mult, op1=ALU.add))
            vdrain()
            op("vector", lambda e: e.tensor_tensor(rs, rs, n1, op=ALU.mult))
        vdrain()
        op("vector", lambda e: e.tensor_scalar(ms, ms, -1.0, None, op0=ALU.mult))
        vdrain()
        xnfm3 = XNFM.ap().rearrange("p (kt t) -> p kt t", t=TC)
        xn_ev = []
        tev_l = []
        xnfm_ev = []
        for tt in range(NTT):
            xt = XTM.ap()[:, 1024 * tt:1024 * (tt + 1)]
            xnslot = XNTM.ap()[:, 1024 * (tt % 2):1024 * (tt % 2 + 1)]
            if tt >= 2:
                wait("vector", "sMM", tev_l[tt - 2])
            mrep = ms[:, tt:tt + 1].broadcast_to([128, 1024])
            rrep = rs[:, tt:tt + 1].broadcast_to([128, 1024])
            op("vector", lambda e, i=xt, m=mrep:
               e.tensor_tensor(SQD, i, m, op=ALU.add))
            E["vector"].append(lambda e: e.drain())
            xn_ev.append(op("vector", lambda e, o=xnslot, r=rrep:
                            e.tensor_tensor(o, SQD, r, op=ALU.mult),
                            inc="sEV"))
            bank = ptr[:, 1024 * (tt % 2):1024 * (tt % 2 + 1)]
            wait("tensor", "sX", t_pre)
            wait("tensor", "sEV", xn_ev[tt])
            if tt >= 2:
                wait("tensor", "sEV", xnfm_ev[tt - 2])
            elif c > 0:
                wait("tensor", "sEV", ev_yev[c - 1][-1])
            tev = None
            for kq in range(8):
                tev = op("tensor", lambda e, o=bank[:, 128 * kq:128 * (kq + 1)],
                         i=xnslot[:, 128 * kq:128 * (kq + 1)]:
                         e.transpose(o, i, IDB.ap()),
                         inc=("sMM" if kq == 7 else None))
            tev_l.append(tev)
            wait("vector", "sMM", tev)
            xnfm_ev.append(op("vector",
                              lambda e, i=bank.rearrange("p (kt t) -> p kt t", t=128),
                              o=xnfm3[:, :, 128 * tt:128 * (tt + 1)]:
                              e.tensor_copy(o, i), inc="sEV"))
        wait("tensor", "sEV", xnfm_ev[-1])
        if _DEBUG and chunk0:
            wait("sync", "sEV", xnfm_ev[-1])
            dma("sync", dbg["xnfm"], XNFM.ap(), "sY")

        def stream_phase(n_strips, cache_ap, f32_loader, evac_fn, first_gate=None):
            evac_tgts = {}
            for g in range(n_strips):
                gi = gstrip[0]
                gstrip[0] += 1
                slot = strip_ring[gi % 3]
                if gi - 3 in strip_consume_ev:
                    wait("gpsimd" if chunk0 else "sync", "sMM",
                         strip_consume_ev[gi - 3])
                if chunk0:
                    arr = f32_loader(g, slot)
                    wait("gpsimd", "sWg", arr[1])
                    dma("gpsimd", cache_ap[g], slot, "sCW")
                else:
                    if g == 0 and first_gate is not None:
                        wait("sync", first_gate[0], first_gate[1])
                    arr = ("sWs", dma("sync", slot, cache_ap[g], "sWs"))
                half = pmm[:, 512 * (g % 2):512 * (g % 2 + 1)]
                wait("tensor", arr[0], arr[1])
                if g >= 2:
                    sname, stgt = evac_tgts[g - 2]
                    wait("tensor", sname, stgt)
                mmev = None
                for kt in range(NKT):
                    mmev = op("tensor",
                              lambda e, o=half, l=slot.rearrange(
                                  "p (kt c) -> p kt c", c=128)[:, kt, :],
                              r=XNFM.ap()[:, TC * kt:TC * (kt + 1)],
                              st=(kt == 0), sp=(kt == NKT - 1):
                              e.matmul(o, l, r, start=st, stop=sp),
                              inc=("sMM" if kt == NKT - 1 else None))
                strip_consume_ev[gi] = mmev
                evac_tgts[g] = evac_fn(g, half, mmev)
            return evac_tgts

        def k_loader(g, slot):
            s3 = slot.rearrange("p (kt c) -> p kt c", c=128)
            dma("gpsimd", s3[:, :, 0:64], wk3[:, :, 64 * g:64 * (g + 1)], "sWg")
            return ("sWg", dma("gpsimd", s3[:, :, 64:128],
                               wk3[:, :, 2048 + 64 * g:2048 + 64 * (g + 1)],
                               "sWg"))

        def k_evac(g, half, mmev):
            wait("vector", "sMM", mmev)
            if g == 0 and c > 0:
                wait("vector", "sMM", last_T_event[0])
            if g == 0:
                t = op("vector", lambda e, i=half:
                       e.tensor_scalar(KACC, i, 0.0, None, op0=ALU.max),
                       inc="sEV")
            else:
                t = op("vector", lambda e, i=half:
                       e.scalar_tensor_tensor(KACC, i, 0.0, KACC,
                                              op0=ALU.max, op1=ALU.add),
                       inc="sEV")
            return ("sEV", t)

        k_ev = stream_phase(NG, ck, k_loader, k_evac)
        wait("scalar", "sEV", k_ev[NG - 1][1])
        t1 = dma("scalar", KPAIR[0:64, :], KACC[64:128, :], "sCons")
        wait("vector", "sCons", t1)
        fold_ev = op("vector", lambda e:
                     e.tensor_tensor(KPAIR[0:64, :], KACC[0:64, :],
                                     KPAIR[0:64, :], op=ALU.add), inc="sEV")
        aug_ev = op("vector", lambda e:
                    e.tensor_copy(vkb[0:64, 64 * TC:65 * TC], KPAIR[0:64, :]),
                    inc="sEV")
        wait("scalar", "sEV", aug_ev)
        kp_dup = dma("scalar", KPAIR[64:128, :], KPAIR[0:64, :], "sCons")

        def q_loader(g, slot):
            s3 = slot.rearrange("p (kt c) -> p kt c", c=128)
            dma("gpsimd", s3[:, :, 0:64], wq3[:, :, 64 * g:64 * (g + 1)], "sWg")
            return ("sWg", dma("gpsimd", s3[:, :, 64:128],
                               wq3[:, :, 2048 + 64 * g:2048 + 64 * (g + 1)],
                               "sWg"))

        def q_evac(g, half, mmev):
            wait("scalar", "sMM", mmev)
            if g == 0 and c > 0:
                wait("scalar", "sMM", last_T_event[0])
            t = op("scalar", lambda e, i=half, o=qbf[:, TC * g:TC * (g + 1)]:
                   e.activation(o, i, AF.Relu, scale=QSCALE), inc="sSE")
            return ("sSE", t)

        q_ev = stream_phase(NG, cq, q_loader, q_evac)
        wait("scalar", "sSE", q_ev[NG - 1][1])
        q_cons = dma("scalar", qbf[0:64, 32 * TC:64 * TC], qbf[64:128, 0:32 * TC],
                     "sCons")

        def v_loader(g, slot):
            s3 = slot.rearrange("p (kt c) -> p kt c", c=128)
            dma("gpsimd", s3[:, :, 0:64], wv3[:, :, 64 * g:64 * (g + 1)], "sWg")
            return ("sWg", dma("gpsimd", s3[:, :, 64:128],
                               wv3[:, :, 2048 + 64 * g:2048 + 64 * (g + 1)],
                               "sWg"))

        def v_evac(g, half, mmev):
            wait("vector", "sMM", mmev)
            if g == 0:
                wait("vector", "sCons", kp_dup)
                if c > 0:
                    wait("vector", "sY", 16 * 4 * c)
            t = op("vector", lambda e, i=half, o=vkb[:, TC * g:TC * (g + 1)]:
                   e.tensor_tensor(o, i, KPAIR, op=ALU.mult), inc="sEV")
            return ("sEV", t)

        v_ev = stream_phase(NG, cv, v_loader, v_evac)
        wait("scalar", "sEV", v_ev[NG - 1][1])
        vk_cons = dma("scalar", vkb[0:64, 32 * TC:64 * TC], vkb[64:128, 0:32 * TC],
                      "sCons")

        if _DEBUG and chunk0:
            wait("sync", "sCons", q_cons)
            wait("sync", "sCons", vk_cons)
            dma("sync", dbg["qbuf"], qbf, "sY")
            dma("sync", dbg["vkbuf"], vkb, "sY")

        wait("tensor", "sCons", q_cons)
        wait("tensor", "sCons", vk_cons)
        if c > 0:
            wait("tensor", "sSE", last_S_event[0])
        e_mm = [None] * NB
        e_div = [None] * NB
        e_tb = [None] * NTB
        e_ev2_v = [None] * NTB
        e_ev2_s = [None] * NTB

        def emit_pt_mms(b):
            par = b % 2
            base = ppt[:, 512 * par: 512 * par + 260]
            if b >= 2:
                wait("tensor", "sEV", e_div[b - 2])
            ev = None
            for s4 in range(4):
                pr = 2 * b + (s4 // 2)
                hoff = HALF * (s4 % 2)
                tA = 2 * pr + hoff
                tB = tA + 1
                for u, tk in enumerate((tA, tB)):
                    o = base[64 * u:64 * (u + 1), 65 * s4: 65 * (s4 + 1)]
                    l = Q3[:, :, tk]
                    r = VK3[:, :, tk]
                    last = (s4 == 3 and u == 1)
                    ev = op("tensor",
                            lambda e, o=o, l=l, r=r, tp=(0, 64 * u):
                            e.matmul(o, l, r, start=True, stop=True,
                                     tile_position=tp, skip_group_check=True),
                            inc=("sMM" if last else None))
            e_mm[b] = ev

        def emit_pt_round(b):
            par = b % 2
            b3 = ppt[:, 512 * par: 512 * par + 260]\
                .rearrange("p (s j) -> p s j", j=65)
            wait("vector", "sMM", e_mm[b])
            op("vector", lambda e, i=b3[:, :, 64]:
               e.tensor_scalar(RECTMP.ap(), i, ATT_EPS, None, op0=ALU.add))
            E["vector"].append(lambda e: e.drain())
            if b >= 1:
                emit_div(b - 1)
            rslot = RECIPB.ap()[:, 4 * (b % 2):4 * (b % 2 + 1)]
            op("vector", lambda e, o=rslot: e.reciprocal(o, RECTMP.ap()))
            E["vector"].append(lambda e: e.drain())

        def emit_div(b):
            par = b % 2
            b3 = ppt[:, 512 * par: 512 * par + 260]\
                .rearrange("p (s j) -> p s j", j=65)
            R = b // 4
            if R >= 2:
                wait("vector", "sMM", e_tb[R - 2])
            rslot = RECIPB.ap()[:, 4 * (b % 2):4 * (b % 2 + 1)]
            stg = STG.ap()[:, 1024 * (R % 2) + 256 * (b % 4):
                           1024 * (R % 2) + 256 * (b % 4 + 1)]
            e_div[b] = op("vector",
                          lambda e, o=stg.rearrange("p (s j) -> p s j", j=64),
                          i=b3[:, :, 0:64],
                          r=rslot.unsqueeze(2).broadcast_to([128, 4, 64]):
                          e.tensor_tensor(o, i, r, op=ALU.mult), inc="sEV")

        def emit_pt_transposes(b):
            R = b // 4
            bank = ptr[:, 1024 * (R % 2):1024 * (R % 2 + 1)]
            wait("tensor", "sEV", e_div[b])
            if b % 4 == 0:
                if R >= 2:
                    wait("tensor", "sSE", e_ev2_v[R - 2])
                elif c > 0 and R == 0:
                    pass
            stg = STG.ap()[:, 1024 * (R % 2) + 256 * (b % 4):
                           1024 * (R % 2) + 256 * (b % 4 + 1)]
            ev = None
            for m in range(2):
                last = (b % 4 == 3 and m == 1)
                ev = op("tensor",
                        lambda e, o=bank[:, 256 * (b % 4) + 128 * m:
                                         256 * (b % 4) + 128 * (m + 1)],
                        i=stg[:, 128 * m:128 * (m + 1)]:
                        e.transpose(o, i, IDB.ap()),
                        inc=("sMM" if last else None))
            if b % 4 == 3:
                e_tb[R] = ev

        def emit_ev2(R):
            bank = ptr[:, 1024 * (R % 2):1024 * (R % 2 + 1)]
            src = [bank[0:64, :].rearrange("p (m u i) -> p m u i", u=2, i=64)
                   .transpose([0, 3, 1, 2]),
                   bank[64:128, :].rearrange("p (m u i) -> p m u i", u=2, i=64)
                   .transpose([0, 3, 1, 2])]
            dst = [obf[0:64, :].rearrange("p (i t) -> p i t", t=HALF)
                   [:, :, 16 * R:16 * (R + 1)]
                   .rearrange("p i (m u) -> p i m u", u=2),
                   obf[64:128, :].rearrange("p (i t) -> p i t", t=HALF)
                   [:, :, 16 * R:16 * (R + 1)]
                   .rearrange("p i (m u) -> p i m u", u=2)]
            wait("scalar", "sMM", e_tb[R])
            if c > 0 and R == 0:
                wait("scalar", "sMM", last_T_event[0])
            e_ev2_s[R] = op("scalar",
                            lambda e, o=dst[0], i=src[0]: e.copy(o, i), inc="sSE")
            e_ev2_v[R] = op("scalar",
                            lambda e, o=dst[1], i=src[1]: e.copy(o, i), inc="sSE")

        for b in range(NB):
            emit_pt_mms(b)
            emit_pt_round(b)
            if b >= 1:
                emit_pt_transposes(b - 1)
            if b >= 1 and (b - 1) % 4 == 3:
                emit_ev2((b - 1) // 4)
        emit_div(NB - 1)
        emit_pt_transposes(NB - 1)
        emit_ev2(NTB - 1)
        pt_end_T = e_tb[NTB - 1]

        e_wo = [None] * 8
        e_woev = [None] * 8
        slab_arr = [None] * 8

        def emit_slab_dma(ct):
            slab = WOSLAB[ct % 2]
            dq = "gpsimd" if chunk0 else "sync"
            wait(dq, "sMM", pt_end_T if ct < 2 else e_wo[ct - 2])
            if chunk0:
                t_cast = dma("gpsimd",
                             slab[0:64, :].rearrange("p (i c) -> p i c", c=128),
                             wo3[:, :, 128 * ct:128 * (ct + 1)], "sWg")
                wait("gpsimd", "sWg", t_cast)
                t_dup = dma("gpsimd", slab[64:128, :], slab[0:64, :], "sWg")
                slab_arr[ct] = ("sWg", t_dup)
                wait("gpsimd", "sWg", t_dup)
                dma("gpsimd", cwo[ct], slab, "sCW")
            else:
                slab_arr[ct] = ("sWs", dma("sync", slab, cwo[ct], "sWs"))

        emit_slab_dma(0)
        emit_slab_dma(1)
        if _DEBUG and chunk0:
            wait("sync", "sSE", e_ev2_v[NTB - 1])
            dma("sync", dbg["outb"], obf, "sY")
        wait("tensor", "sSE", e_ev2_v[NTB - 1])
        for ct in range(8):
            slab = WOSLAB[ct % 2]
            s3 = slab.rearrange("p (i c) -> p i c", c=128)
            wait("tensor", slab_arr[ct][0], slab_arr[ct][1])
            if ct >= 2:
                wait("tensor", "sSE", e_woev[ct - 2])
            bank_h1 = pwo[:, 0:256] if ct % 2 == 0 else ppt[:, 0:256]
            bank_h2 = pwo[:, 512:768] if ct % 2 == 0 else ppt[:, 512:768]
            ev = None
            for i in range(64):
                st = (i == 0)
                sp = (i == 63)
                op("tensor", lambda e, o=bank_h1, l=s3[0:64, i, :],
                   r=OV_I[0][:, i, :], st=st, sp=sp:
                   e.matmul(o, l, r, start=st, stop=sp, skip_group_check=True))
                ev = op("tensor", lambda e, o=bank_h2, l=s3[64:128, i, :],
                        r=OV_I[1][:, i, :], st=st, sp=sp:
                        e.matmul(o, l, r, start=st, stop=sp,
                                 skip_group_check=True),
                        inc=("sMM" if sp else None))
            e_wo[ct] = ev
            wait("scalar", "sMM", ev)
            op("scalar", lambda e, i=bank_h1, o=YFM[:, 512 * ct:512 * ct + 256],
               b=BOv.ap()[:, ct:ct + 1]:
               e.activation(o, i, AF.Identity, bias=b))
            e_woev[ct] = op("scalar",
                            lambda e, i=bank_h2,
                            o=YFM[:, 512 * ct + 256:512 * ct + 512],
                            b=BOv.ap()[:, ct:ct + 1]:
                            e.activation(o, i, AF.Identity, bias=b), inc="sSE")
            if ct + 2 < 8:
                emit_slab_dma(ct + 2)

        ffn_in = [XNFM.ap(), H1FM, H2FM]
        ffn_cache = [cwf, ca1, ca2]
        ffn_f32 = [wf3, a13, a23]
        e_ffn = [[None] * 8 for _ in range(3)]
        e_fev = [[None] * 8 for _ in range(3)]
        tanh_ev = [None] * 8
        add_ev = [None] * 8
        fstrip_consumed = {}
        fidx = [0]
        for s in range(3):
            for g in range(8):
                fi = fidx[0]
                fidx[0] += 1
                slot = FSTRIP[fi % 3]
                dq = "gpsimd" if chunk0 else "sync"
                if fi < 3:
                    wait(dq, "sMM", pt_end_T)
                else:
                    wait(dq, "sMM", fstrip_consumed[fi - 3])
                if chunk0:
                    dma("gpsimd", slot.rearrange("p (kt c) -> p kt c", c=128),
                        ffn_f32[s][:, :, 128 * g:128 * (g + 1)], "sWg")
                    arr = ("sWg", C["sWg"])
                    wait("gpsimd", "sWg", arr[1])
                    dma("gpsimd", ffn_cache[s][g], slot, "sCW")
                else:
                    arr = ("sWs", dma("sync", slot, ffn_cache[s][g], "sWs"))
                half = pmm[:, 512 * (g % 2):512 * (g % 2 + 1)]
                wait("tensor", arr[0], arr[1])
                if s == 0 and g < 2:
                    pass
                if g >= 2 or s > 0:
                    pg = (s, g - 2) if g >= 2 else (s - 1, g + 6)
                    sname = "sSE" if pg[0] == 2 else "sEV"
                    tgtl = e_fev[pg[0]][pg[1]]
                    wait("tensor", sname, tgtl)
                if s > 0 and g == 0:
                    wait("tensor", "sEV", e_fev[s - 1][7])
                mmev = None
                for kt in range(NKT):
                    mmev = op("tensor",
                              lambda e, o=half,
                              l=slot.rearrange("p (kt c) -> p kt c", c=128)[:, kt, :],
                              r=ffn_in[s][:, TC * kt:TC * (kt + 1)],
                              st=(kt == 0), sp=(kt == NKT - 1):
                              e.matmul(o, l, r, start=st, stop=sp),
                              inc=("sMM" if kt == NKT - 1 else None))
                fstrip_consumed[fi] = mmev
                e_ffn[s][g] = mmev
                if s < 2:
                    dst = (H1FM if s == 0 else H2FM)
                    wait("vector", "sMM", mmev)
                    e_fev[s][g] = op("vector",
                                     lambda e, i=half, o=dst[:, TC * g:TC * (g + 1)]:
                                     e.tensor_copy(o, i), inc="sEV")
                else:
                    wait("scalar", "sMM", mmev)
                    if g >= 2:
                        wait("scalar", "sEV", add_ev[g - 2])
                    h3 = H3TMP[g % 2]
                    tanh_ev[g] = op("scalar",
                                    lambda e, i=half, o=h3, b=FBv.ap()[:, g:g + 1]:
                                    e.activation(o, i, AF.Tanh, bias=b), inc="sSE")
                    e_fev[s][g] = tanh_ev[g]
                    wait("vector", "sSE", tanh_ev[g])
                    add_ev[g] = op("vector",
                                   lambda e, a=YFM[:, TC * g:TC * (g + 1)], h=h3:
                                   e.tensor_tensor(a, a, h, op=ALU.add), inc="sEV")

        if _DEBUG and chunk0:
            wait("sync", "sEV", add_ev[7])
            dma("sync", dbg["yfm"], YFM, "sY")

        yev_list = []
        y_dma_tgts = {}
        ev_prev_tile = None
        for tt in range(NTT):
            ytm = YTM[tt % 2]
            yevs = []
            for hb in range(2):
                bank = pmm[:, 512 * hb:512 * (hb + 1)]
                if tt == 0 and hb == 0:
                    wait("tensor", "sSE", tanh_ev[7])
                    wait("tensor", "sEV", add_ev[7])
                if tt >= 1:
                    wait("tensor", "sEV", ev_prev_tile)
                tev = None
                for kq in range(4):
                    kt = 4 * hb + kq
                    tev = op("tensor",
                             lambda e, o=bank[:, 128 * kq:128 * (kq + 1)],
                             i=YFM[:, TC * kt + 128 * tt:TC * kt + 128 * (tt + 1)]:
                             e.transpose(o, i, IDF.ap()),
                             inc=("sMM" if kq == 3 else None))
                wait("vector", "sMM", tev)
                if tt >= 2:
                    wait("vector", "sY", y_dma_tgts[tt - 2])
                yev = op("vector",
                         lambda e, o=ytm[:, 512 * hb:512 * (hb + 1)], i=bank,
                         x=XTM.ap()[:, 1024 * tt + 512 * hb:1024 * tt + 512 * (hb + 1)]:
                         e.scalar_tensor_tensor(o, i, 0.0, x,
                                                op0=ALU.bypass, op1=ALU.add),
                         inc="sEV")
                yevs.append(yev)
            ev_prev_tile = yevs[-1]
            yev_list.extend(yevs)
            wait("sync", "sEV", yevs[-1])
            y_dma_tgts[tt] = dma(
                "sync", dy[tok0 + 128 * tt: tok0 + 128 * (tt + 1), :], ytm, "sY")
        ev_yev.append(yev_list)
        last_T_event[0] = C["sMM"]
        last_S_event[0] = C["sSE"]
        if chunk0:
            cache_done = C["sCW"]
            wait("sync", "sCW", cache_done)

    wait("sync", "sY", C["sY"])

    with nc.Block() as block:
        @block.sync
        def _(eng):
            for f in E["sync"]:
                f(eng)

        @block.gpsimd
        def _(eng):
            for f in E["gpsimd"]:
                f(eng)

        @block.tensor
        def _(eng):
            for f in E["tensor"]:
                f(eng)

        @block.vector
        def _(eng):
            for f in E["vector"]:
                f(eng)

        @block.scalar
        def _(eng):
            for f in E["scalar"]:
                f(eng)

    return nc


_NC_CACHE = None


def kernel(L0_attn_x, L0_attn_Wq, L0_attn_Wk, L0_attn_Wv, L0_attn_Wo,
           L0_attn_bo, L0_ffn_weight, L0_ffn_aux_weight1, L0_ffn_aux_weight2,
           L0_ffn_bias):
    global _NC_CACHE
    x = np.ascontiguousarray(np.asarray(L0_attn_x, dtype=np.float32))
    wq = np.ascontiguousarray(np.asarray(L0_attn_Wq, dtype=np.float32))
    wk = np.ascontiguousarray(np.asarray(L0_attn_Wk, dtype=np.float32))
    wv = np.ascontiguousarray(np.asarray(L0_attn_Wv, dtype=np.float32))
    wo = np.ascontiguousarray(np.asarray(L0_attn_Wo, dtype=np.float32))
    bo = np.ascontiguousarray(np.asarray(L0_attn_bo, dtype=np.float32))
    wf = np.ascontiguousarray(np.asarray(L0_ffn_weight, dtype=np.float32))
    a1 = np.ascontiguousarray(np.asarray(L0_ffn_aux_weight1, dtype=np.float32))
    a2 = np.ascontiguousarray(np.asarray(L0_ffn_aux_weight2, dtype=np.float32))
    fb = np.ascontiguousarray(np.asarray(L0_ffn_bias, dtype=np.float32))
    idb = np.eye(128, dtype=np.float32).astype(ml_dtypes.bfloat16)
    idf = np.eye(128, dtype=np.float32)

    if _NC_CACHE is None:
        _NC_CACHE = build_nc()
    nc = _NC_CACHE
    common = {"wq": wq, "wk": wk, "wv": wv, "wo": wo, "bo": bo,
              "wf": wf, "a1": a1, "a2": a2, "fb": fb, "idb": idb, "idf": idf}
    in_maps = [dict(common, x=x[c]) for c in range(NCORES)]
    res = run_bass_kernel_spmd(nc, in_maps, core_ids=list(range(NCORES)))
    if _DEBUG:
        for k in ["dbg_xnfm", "dbg_qbuf", "dbg_vkbuf", "dbg_outb", "dbg_yfm"]:
            np.save("/tmp/%s.npy" % k,
                    np.asarray(res.results[0][k]).astype(np.float32))
        np.save("/tmp/dbg_outb_all.npy",
                np.stack([np.asarray(res.results[c]["dbg_outb"]).astype(np.float32)
                          for c in range(NCORES)]))
        np.save("/tmp/dbg_vkbuf_all.npy",
                np.stack([np.asarray(res.results[c]["dbg_vkbuf"]).astype(np.float32)
                          for c in range(NCORES)]))
        np.save("/tmp/dbg_qbuf_all.npy",
                np.stack([np.asarray(res.results[c]["dbg_qbuf"]).astype(np.float32)
                          for c in range(NCORES)]))
    y = np.stack([np.asarray(res.results[c]["y"]) for c in range(NCORES)])
    return y


# version 08
# speedup vs baseline: 1.1486x; 1.0203x over previous
import os
import numpy as np
import ml_dtypes
import concourse.bass as bass
import concourse.mybir as mybir
from concourse.bass_utils import run_bass_kernel_spmd

_DEBUG = os.environ.get("BASSDBG", "") == "1"
_DBG_CHUNK = int(os.environ.get("BASSDBG_CHUNK", "0"))

F32 = mybir.dt.float32
BF16 = mybir.dt.bfloat16
AF = mybir.ActivationFunctionType
ALU = mybir.AluOpType
AX = mybir.AxisListType

NCORES = 8
T = 2048
D = 1024
NH = 64
DH = 64
INNER = 4096
TC = 512
NCH = T // TC
NTT = TC // 128
NKT = D // 128
NG = 32
NB = TC // 8
NTB = NB // 4
HALF = TC // 2
LN_EPS = 1e-5
ATT_EPS = 1e-6
QSCALE = 0.125


def build_nc():
    nc = bass.Bass("TRN2", target_bir_lowering=False)

    dx = nc.dram_tensor("x", [T, D], F32, kind="ExternalInput").ap()
    dwq = nc.dram_tensor("wq", [D, INNER], F32, kind="ExternalInput").ap()
    dwk = nc.dram_tensor("wk", [D, INNER], F32, kind="ExternalInput").ap()
    dwv = nc.dram_tensor("wv", [D, INNER], F32, kind="ExternalInput").ap()
    dwo = nc.dram_tensor("wo", [INNER, D], F32, kind="ExternalInput").ap()
    dwf = nc.dram_tensor("wf", [D, D], F32, kind="ExternalInput").ap()
    da1 = nc.dram_tensor("a1", [D, D], F32, kind="ExternalInput").ap()
    da2 = nc.dram_tensor("a2", [D, D], F32, kind="ExternalInput").ap()
    dbo = nc.dram_tensor("bo", [D], F32, kind="ExternalInput").ap()
    dfb = nc.dram_tensor("fb", [D], F32, kind="ExternalInput").ap()
    didb = nc.dram_tensor("idb", [128, 128], BF16, kind="ExternalInput").ap()
    didf = nc.dram_tensor("idf", [128, 128], F32, kind="ExternalInput").ap()
    dy = nc.dram_tensor("y", [T, D], F32, kind="ExternalOutput").ap()
    cq = nc.dram_tensor("cq", [NG, 128, 1024], BF16, kind="Internal").ap()
    ck = nc.dram_tensor("ck", [NG, 128, 1024], BF16, kind="Internal").ap()
    cv = nc.dram_tensor("cv", [NG, 128, 1024], BF16, kind="Internal").ap()
    cwo = nc.dram_tensor("cwo", [8, 128, 8192], BF16, kind="Internal").ap()
    cwf = nc.dram_tensor("cwf", [8, 128, 1024], BF16, kind="Internal").ap()
    ca1 = nc.dram_tensor("ca1", [8, 128, 1024], BF16, kind="Internal").ap()
    ca2 = nc.dram_tensor("ca2", [8, 128, 1024], BF16, kind="Internal").ap()
    dbg = {}
    if _DEBUG:
        dbg["xnfm"] = nc.dram_tensor("dbg_xnfm", [128, 4096], BF16,
                                     kind="ExternalOutput").ap()
        dbg["qbuf"] = nc.dram_tensor("dbg_qbuf", [128, 32768], BF16,
                                     kind="ExternalOutput").ap()
        dbg["vkbuf"] = nc.dram_tensor("dbg_vkbuf", [128, 33280], BF16,
                                      kind="ExternalOutput").ap()
        dbg["outb"] = nc.dram_tensor("dbg_outb", [128, 16384], BF16,
                                     kind="ExternalOutput").ap()
        dbg["yfm"] = nc.dram_tensor("dbg_yfm", [128, 4096], F32,
                                    kind="ExternalOutput").ap()
        dbg["stats"] = nc.dram_tensor("dbg_stats", [128, 16], F32,
                                      kind="ExternalOutput").ap()
        dbg["kpair"] = nc.dram_tensor("dbg_kpair", [128, 512], F32,
                                      kind="ExternalOutput").ap()
        dbg["cv"] = nc.dram_tensor("dbg_cv", [NG, 128, 1024], BF16,
                                   kind="ExternalOutput").ap()
        dbg["xnfm2"] = nc.dram_tensor("dbg_xnfm2", [128, 4096], BF16,
                                      kind="ExternalOutput").ap()

    QBUF = nc.alloc_sbuf_tensor("QBUF", [128, 64 * TC], BF16)
    VKBUF = nc.alloc_sbuf_tensor("VKBUF", [128, 65 * TC], BF16)
    OUTB = nc.alloc_sbuf_tensor("OUTB", [128, HALF * 64], BF16)
    XTM = nc.alloc_sbuf_tensor("XTM", [128, 2 * 1024], F32)
    XR = nc.alloc_sbuf_tensor("XR", [128, 2 * 1024], F32)
    XNTM = nc.alloc_sbuf_tensor("XNTM", [128, 2 * 1024], BF16)
    XNFM = nc.alloc_sbuf_tensor("XNFM", [128, NKT * TC], BF16)
    WSTRIP = nc.alloc_sbuf_tensor("WSTRIP", [128, 4 * 1024], BF16)
    STG = nc.alloc_sbuf_tensor("STG", [128, 2 * 1024], BF16)
    RECIPB = nc.alloc_sbuf_tensor("RECIPB", [128, 8], F32)
    RECTMP = nc.alloc_sbuf_tensor("RECTMP", [128, 4], F32)
    MEANS = nc.alloc_sbuf_tensor("MEANS", [128, 4], F32)
    SUMSQ = nc.alloc_sbuf_tensor("SUMSQ", [128, 4], F32)
    VARB = nc.alloc_sbuf_tensor("VARB", [128, 4], F32)
    RSTD = nc.alloc_sbuf_tensor("RSTD", [128, 4], F32)
    NT1 = nc.alloc_sbuf_tensor("NT1", [128, 4], F32)
    BOv = nc.alloc_sbuf_tensor("BOv", [128, 8], F32)
    FBv = nc.alloc_sbuf_tensor("FBv", [128, 8], F32)
    IDB = nc.alloc_sbuf_tensor("IDB", [128, 128], BF16)
    IDF = nc.alloc_sbuf_tensor("IDF", [128, 128], F32)

    obf = OUTB.ap()
    KACC = obf[:, 0:1024].bitcast(F32)
    KPAIRT = nc.alloc_sbuf_tensor("KPAIRT", [128, 512], F32)
    VTMP = nc.alloc_sbuf_tensor("VTMP", [128, 1024], BF16)
    SQD = obf[:, 2048:4096].bitcast(F32)
    SQD2 = STG.ap()[:, 0:2048].bitcast(F32)
    qbf = QBUF.ap()
    WOSLAB = [qbf[:, 0:8192], qbf[:, 8192:16384]]
    YFM = qbf[:, 16384:24576].bitcast(F32)
    H1FM = qbf[:, 24576:28672]
    vkb = VKBUF.ap()
    H2FM = vkb[:, 0:4096]
    FSTRIP = [vkb[:, 4096 + 1024 * s: 4096 + 1024 * (s + 1)] for s in range(3)]
    H3TMP = [vkb[:, 7168:8192].bitcast(F32), vkb[:, 8192:9216].bitcast(F32)]
    YTM = [vkb[:, 9216 + 2048 * s: 9216 + 2048 * (s + 1)].bitcast(F32)
           for s in range(2)]

    Q3 = qbf[0:64, :].rearrange("p (i t) -> p i t", t=TC)
    VK3 = vkb[0:64, :].rearrange("p (i t) -> p i t", t=TC)
    OV_I = [obf[0:64, :].rearrange("p (i t) -> p i t", t=HALF),
            obf[64:128, :].rearrange("p (i t) -> p i t", t=HALF)]

    PMM = nc.alloc_psum_tensor("PMM", [128, 1024], F32)
    PPT = nc.alloc_psum_tensor("PPT", [128, 1024], F32)
    PTR = nc.alloc_psum_tensor("PTR", [128, 2048], BF16)
    PWO = nc.alloc_psum_tensor("PWO", [128, 1024], F32)
    pmm = PMM.ap()
    ppt = PPT.ap()
    ptr = PTR.ap()
    pwo = PWO.ap()

    sem_names = ["sX", "sWg", "sWs", "sCW", "sCons", "sMM", "sEV", "sSE", "sY"]
    sems = {n: nc.alloc_semaphore(name=n) for n in sem_names}
    C = {n: 0 for n in sem_names}
    snums = sorted(s.num for s in sems.values())
    assert snums[-1] - snums[0] == len(snums) - 1
    nc.gpsimd.sem_clear(range(snums[0], snums[-1] + 1))
    nc.all_engine_barrier()

    E = {"sync": [], "gpsimd": [], "tensor": [], "vector": [], "scalar": []}

    def wait(eng, sem, tgt):
        if tgt > 0:
            E[eng].append(lambda e, s=sems[sem], t=tgt: e.wait_ge(s, t))

    def dma(eng, out, in_, sem, slow=False):
        C[sem] += 16
        tgt = C[sem]
        kw = {"allow_slow_non_contiguous": True} if slow else {}
        E[eng].append(
            lambda e, o=out, i=in_, s=sems[sem], kw=kw: e.dma_start(
                out=o, in_=i, **kw).then_inc(s, 16))
        return tgt

    def op(eng, fn, inc=None):
        if inc is None:
            E[eng].append(fn)
            return None
        C[inc] += 1
        tgt = C[inc]
        E[eng].append(lambda e, f=fn, s=sems[inc]: f(e).then_inc(s, 1))
        return tgt

    dma("sync", BOv.ap(), dbo.rearrange("(a p) -> p a", p=128), "sX", slow=True)
    dma("sync", FBv.ap(), dfb.rearrange("(a p) -> p a", p=128), "sX", slow=True)
    dma("sync", IDB.ap(), didb, "sX")
    t_pre = dma("sync", IDF.ap(), didf, "sX")

    wq3 = dwq.rearrange("(kt p) c -> p kt c", p=128)
    wk3 = dwk.rearrange("(kt p) c -> p kt c", p=128)
    wv3 = dwv.rearrange("(kt p) c -> p kt c", p=128)
    wo3 = dwo.rearrange("(i p) c -> p i c", p=64)
    wf3 = dwf.rearrange("(kt p) c -> p kt c", p=128)
    a13 = da1.rearrange("(kt p) c -> p kt c", p=128)
    a23 = da2.rearrange("(kt p) c -> p kt c", p=128)

    ev_yev = []
    last_T_event = [0]
    last_S_event = [0]

    strip_ring = [WSTRIP.ap()[:, 1024 * s: 1024 * (s + 1)] for s in range(4)]
    strip_pair = [WSTRIP.ap()[:, 2048 * s: 2048 * (s + 1)] for s in range(2)]
    strip_consume_ev = {}
    cw_tgt = {}
    gstrip = [0]

    A = {cc: {"x_tgts": {}, "xn_ev": {}, "tev": {}, "xnfm_ev": []}
         for cc in range(NCH)}
    xnfm3 = XNFM.ap().rearrange("p (kt t) -> p kt t", t=TC)
    ms = MEANS.ap()
    sq = SUMSQ.ap()
    vb = VARB.ap()
    rs = RSTD.ap()
    n1 = NT1.ap()

    def vdrain():
        E["vector"].append(lambda e: e.drain())

    def tdrain():
        E["tensor"].append(lambda e: e.drain())

    def op_fenced(eng, fn, inc):
        E[eng].append(fn)
        C[inc] += 1
        tgt = C[inc]
        E[eng].append(lambda e, s=sems[inc]: e.drain().then_inc(s, 1))
        return tgt

    def emit_xload(cc, tt):
        if tt >= 2:
            wait("sync", "sEV", A[cc]["xn_ev"][tt - 2])
        elif cc > 0:
            wait("sync", "sEV", A[cc - 1]["xn_ev"][tt + 2])
        A[cc]["x_tgts"][tt] = dma(
            "sync", XTM.ap()[:, 1024 * (tt % 2):1024 * (tt % 2 + 1)],
            dx[TC * cc + 128 * tt: TC * cc + 128 * (tt + 1), :], "sX")

    def emit_stats_pair(cc, p, stg_free_tgt):
        if stg_free_tgt is not None:
            wait("vector", "sMM", stg_free_tgt)
        for tt in (2 * p, 2 * p + 1):
            xt = XTM.ap()[:, 1024 * (tt % 2):1024 * (tt % 2 + 1)]
            wait("vector", "sX", A[cc]["x_tgts"][tt])
            op("vector", lambda e, o=ms[:, tt:tt + 1], i=xt:
               e.reduce_sum(o, i, axis=AX.X))
            op("vector", lambda e, i=xt: e.tensor_tensor(SQD2, i, i, op=ALU.mult))
            vdrain()
            op("vector", lambda e, a=sq[:, tt:tt + 1]:
               e.reduce_sum(a, SQD2, axis=AX.X))
            vdrain()
        s = slice(2 * p, 2 * p + 2)
        m2, q2, v2, r2, t2 = ms[:, s], sq[:, s], vb[:, s], rs[:, s], n1[:, s]
        op("vector", lambda e: e.tensor_scalar(m2, m2, 1.0 / D, None, op0=ALU.mult))
        op("vector", lambda e: e.tensor_scalar(q2, q2, 1.0 / D, None, op0=ALU.mult))
        vdrain()
        op("vector", lambda e: e.tensor_tensor(v2, m2, m2, op=ALU.mult))
        vdrain()
        op("vector", lambda e: e.scalar_tensor_tensor(
            v2, v2, -1.0, q2, op0=ALU.mult, op1=ALU.add))
        vdrain()
        op("vector", lambda e: e.tensor_scalar(v2, v2, LN_EPS, None, op0=ALU.add))
        vdrain()
        op("vector", lambda e: e.reciprocal(r2, v2))
        for _ in range(6):
            vdrain()
            op("vector", lambda e: e.tensor_tensor(t2, r2, r2, op=ALU.mult))
            vdrain()
            op("vector", lambda e: e.tensor_tensor(t2, t2, v2, op=ALU.mult))
            vdrain()
            op("vector", lambda e: e.tensor_scalar(
                t2, t2, -0.5, 1.5, op0=ALU.mult, op1=ALU.add))
            vdrain()
            op("vector", lambda e: e.tensor_tensor(r2, r2, t2, op=ALU.mult))
        vdrain()
        op("vector", lambda e: e.tensor_scalar(m2, m2, -1.0, None, op0=ALU.mult))
        vdrain()

    def emit_xn(cc, tt):
        xt = XTM.ap()[:, 1024 * (tt % 2):1024 * (tt % 2 + 1)]
        xnslot = XNTM.ap()[:, 1024 * (tt % 2):1024 * (tt % 2 + 1)]
        if tt >= 2:
            wait("vector", "sMM", A[cc]["tev"][tt - 2])
        elif cc > 0:
            wait("vector", "sMM", A[cc - 1]["tev"][tt + 2])
        mrep = ms[:, tt:tt + 1].broadcast_to([128, 1024])
        rrep = rs[:, tt:tt + 1].broadcast_to([128, 1024])
        op("vector", lambda e, i=xt, m=mrep:
           e.tensor_tensor(SQD, i, m, op=ALU.add))
        E["vector"].append(lambda e: e.drain())
        A[cc]["xn_ev"][tt] = op_fenced(
            "vector", lambda e, o=xnslot, r=rrep:
            e.tensor_tensor(o, SQD, r, op=ALU.mult), "sEV")
        bank = ptr[:, 1024 * (tt % 2):1024 * (tt % 2 + 1)]
        wait("tensor", "sX", t_pre)
        wait("tensor", "sEV", A[cc]["xn_ev"][tt])
        if tt >= 2:
            wait("tensor", "sEV", A[cc]["xnfm_ev"][tt - 2])
        tdrain()
        tev = None
        for kq in range(8):
            tev = op("tensor", lambda e, o=bank[:, 128 * kq:128 * (kq + 1)],
                     i=xnslot[:, 128 * kq:128 * (kq + 1)]:
                     e.transpose(o, i, IDB.ap()),
                     inc=("sMM" if kq == 7 else None))
        A[cc]["tev"][tt] = tev
        wait("vector", "sMM", tev)
        A[cc]["xnfm_ev"].append(op_fenced(
            "vector", lambda e, i=bank.rearrange("p (kt t) -> p kt t", t=128),
            o=xnfm3[:, :, 128 * tt:128 * (tt + 1)]:
            e.tensor_copy(o, i), "sEV"))

    last_phase_evacs = [None]
    for c in range(NCH):
        chunk0 = (c == 0)
        tok0 = TC * c

        if chunk0:
            emit_xload(0, 0)
            emit_xload(0, 1)
            emit_stats_pair(0, 0, None)
            emit_xn(0, 0)
            emit_xn(0, 1)
            emit_xload(0, 2)
            emit_xload(0, 3)
            emit_stats_pair(0, 1, None)
            emit_xn(0, 2)
            emit_xn(0, 3)
        xnfm_ev = A[c]["xnfm_ev"]
        wait("tensor", "sEV", xnfm_ev[-1])
        if os.environ.get("BARRIER_TEST") == "1" and c > 0:
            wait("tensor", "sEV", C["sEV"])
            wait("tensor", "sSE", C["sSE"])
            wait("vector", "sMM", C["sMM"])
            wait("vector", "sSE", C["sSE"])
            wait("scalar", "sMM", C["sMM"])
            wait("scalar", "sEV", C["sEV"])
        tdrain()
        if _DEBUG and c == _DBG_CHUNK:
            wait("sync", "sEV", xnfm_ev[-1])
            dma("sync", dbg["xnfm"], XNFM.ap(), "sY")
            dma("sync", dbg["stats"][:, 0:4], ms, "sY")
            dma("sync", dbg["stats"][:, 4:8], rs, "sY")
            dma("sync", dbg["stats"][:, 8:12], vb, "sY")
            stats_dump_tgt = dma("sync", dbg["stats"][:, 12:16], sq, "sY")

        def stream_phase(n_strips, cache_ap, f32_loader, evac_fn, first_gate=None,
                         prev_evacs=None):
            evac_tgts = {}
            for g in range(n_strips):
                gi = gstrip[0]
                gstrip[0] += 1
                slot = strip_ring[gi % 4]
                if gi - 4 in strip_consume_ev:
                    wait("gpsimd" if chunk0 else "sync", "sMM",
                         strip_consume_ev[gi - 4])
                if chunk0 and gi >= 4:
                    wait("gpsimd", "sCW", cw_tgt[gi - 4])
                if chunk0:
                    arr = f32_loader(g, slot)
                    wait("gpsimd", "sWg", arr[1])
                    cw_tgt[gi] = dma("gpsimd", cache_ap[g], slot, "sCW")
                else:
                    if g == 0 and first_gate is not None:
                        wait("sync", first_gate[0], first_gate[1])
                    arr = ("sWs", dma("sync", slot, cache_ap[g], "sWs"))
                half = pmm[:, 512 * (g % 2):512 * (g % 2 + 1)]
                wait("tensor", arr[0], arr[1])
                if g >= 2:
                    sname, stgt = evac_tgts[g - 2]
                    wait("tensor", sname, stgt)
                elif prev_evacs is not None:
                    sname, stgt = prev_evacs[g]
                    wait("tensor", sname, stgt)
                mmev = None
                for kt in range(NKT):
                    mmev = op("tensor",
                              lambda e, o=half, l=slot.rearrange(
                                  "p (kt c) -> p kt c", c=128)[:, kt, :],
                              r=XNFM.ap()[:, TC * kt:TC * (kt + 1)],
                              st=(kt == 0), sp=(kt == NKT - 1):
                              e.matmul(o, l, r, start=st, stop=sp),
                              inc=("sMM" if kt == NKT - 1 else None))
                strip_consume_ev[gi] = mmev
                evac_tgts[g] = evac_fn(g, half, mmev)
            return evac_tgts

        def k_loader(g, slot):
            s3 = slot.rearrange("p (kt c) -> p kt c", c=128)
            dma("gpsimd", s3[:, :, 0:64], wk3[:, :, 64 * g:64 * (g + 1)], "sWg")
            return ("sWg", dma("gpsimd", s3[:, :, 64:128],
                               wk3[:, :, 2048 + 64 * g:2048 + 64 * (g + 1)],
                               "sWg"))

        def k_evac(g, half, mmev):
            wait("vector", "sMM", mmev)
            if g == 0:
                t = op("vector", lambda e, i=half:
                       e.tensor_scalar(KACC, i, 0.0, None, op0=ALU.max),
                       inc="sEV")
            else:
                t = op("vector", lambda e, i=half:
                       e.scalar_tensor_tensor(KACC, i, 0.0, KACC,
                                              op0=ALU.max, op1=ALU.add),
                       inc="sEV")
            return ("sEV", t)

        k_ev = stream_phase(NG, ck, k_loader, k_evac,
                            prev_evacs=last_phase_evacs[0])
        k_ev2 = [k_ev[NG - 2], k_ev[NG - 1]]
        wait("scalar", "sEV", k_ev[NG - 1][1])
        t1 = dma("scalar", KPAIRT.ap()[0:64, :], KACC[64:128, :], "sCons")
        wait("vector", "sCons", t1)
        fold_ev = op("vector", lambda e:
                     e.tensor_tensor(KPAIRT.ap()[0:64, :], KACC[0:64, :],
                                     KPAIRT.ap()[0:64, :], op=ALU.add), inc="sEV")
        aug_ev = op("vector", lambda e:
                    e.tensor_copy(vkb[0:64, 64 * TC:65 * TC], KPAIRT.ap()[0:64, :]),
                    inc="sEV")
        wait("scalar", "sEV", aug_ev)
        kp_dup = dma("scalar", KPAIRT.ap()[64:128, :], KPAIRT.ap()[0:64, :], "sCons")

        def q_loader(g, slot):
            s3 = slot.rearrange("p (kt c) -> p kt c", c=128)
            dma("gpsimd", s3[:, :, 0:64], wq3[:, :, 64 * g:64 * (g + 1)], "sWg")
            return ("sWg", dma("gpsimd", s3[:, :, 64:128],
                               wq3[:, :, 2048 + 64 * g:2048 + 64 * (g + 1)],
                               "sWg"))

        def q_evac(g, half, mmev):
            wait("scalar", "sMM", mmev)
            t = op("scalar", lambda e, i=half, o=qbf[:, TC * g:TC * (g + 1)]:
                   e.activation(o, i, AF.Relu, scale=QSCALE), inc="sSE")
            return ("sSE", t)

        q_ev = stream_phase(NG, cq, q_loader, q_evac, prev_evacs=k_ev2)
        q_ev2 = [q_ev[NG - 2], q_ev[NG - 1]]
        wait("scalar", "sSE", q_ev[NG - 1][1])
        q_cons = dma("scalar", qbf[0:64, 32 * TC:64 * TC], qbf[64:128, 0:32 * TC],
                     "sCons")

        def v_loader(g, slot):
            s3 = slot.rearrange("p (kt c) -> p kt c", c=128)
            dma("gpsimd", s3[:, :, 0:64], wv3[:, :, 64 * g:64 * (g + 1)], "sWg")
            return ("sWg", dma("gpsimd", s3[:, :, 64:128],
                               wv3[:, :, 2048 + 64 * g:2048 + 64 * (g + 1)],
                               "sWg"))

        def v_evac(g, half, mmev):
            wait("scalar", "sMM", mmev)
            vslot = VTMP.ap()[:, 512 * (g % 2):512 * (g % 2 + 1)]
            ts = op("scalar", lambda e, i=half, o=vslot:
                    e.copy(o, i), inc="sSE")
            wait("vector", "sSE", ts)
            if g == 0:
                wait("vector", "sCons", kp_dup)
                if c > 0:
                    wait("vector", "sY", 16 * 4 * c)
            t = op("vector", lambda e, i=vslot, o=vkb[:, TC * g:TC * (g + 1)]:
                   e.tensor_tensor(o, i, KPAIRT.ap(), op=ALU.mult), inc="sEV")
            return ("sEV", t)

        v_ev = stream_phase(NG, cv, v_loader, v_evac, prev_evacs=q_ev2)
        wait("scalar", "sEV", v_ev[NG - 1][1])
        vk_cons = dma("scalar", vkb[0:64, 32 * TC:64 * TC], vkb[64:128, 0:32 * TC],
                      "sCons")

        if _DEBUG and c == _DBG_CHUNK:
            wait("sync", "sCons", q_cons)
            wait("sync", "sCons", vk_cons)
            dma("sync", dbg["qbuf"], qbf, "sY")
            dma("sync", dbg["vkbuf"], vkb, "sY")
            dma("sync", dbg["kpair"], KPAIRT.ap(), "sY")

        wait("tensor", "sCons", q_cons)
        wait("tensor", "sCons", vk_cons)
        if c > 0:
            wait("tensor", "sSE", last_S_event[0])
        e_mm = [None] * NB
        e_div = [None] * NB
        e_tb = [None] * NTB
        e_ev2_v = [None] * NTB
        e_ev2_s = [None] * NTB

        def emit_pt_mms(b):
            par = b % 2
            base = ppt[:, 512 * par: 512 * par + 260]
            if b >= 2:
                wait("tensor", "sEV", e_div[b - 2])
            tdrain()
            ev = None
            for s4 in range(4):
                pr = 2 * b + (s4 // 2)
                hoff = HALF * (s4 % 2)
                tA = 2 * pr + hoff
                tB = tA + 1
                for u, tk in enumerate((tA, tB)):
                    o = base[64 * u:64 * (u + 1), 65 * s4: 65 * (s4 + 1)]
                    l = Q3[:, :, tk]
                    r = VK3[:, :, tk]
                    last = (s4 == 3 and u == 1)
                    if last and b == 0:
                        op("tensor",
                           lambda e, o=o, l=l, r=r, tp=(0, 64 * u):
                           e.matmul(o, l, r, start=True, stop=True,
                                    tile_position=tp, skip_group_check=True))
                        C["sMM"] += 1
                        ev = C["sMM"]
                        E["tensor"].append(
                            lambda e, s=sems["sMM"]: e.drain().then_inc(s, 1))
                    else:
                        ev = op("tensor",
                                lambda e, o=o, l=l, r=r, tp=(0, 64 * u):
                                e.matmul(o, l, r, start=True, stop=True,
                                         tile_position=tp, skip_group_check=True),
                                inc=("sMM" if last else None))
            e_mm[b] = ev

        def emit_pt_round(b):
            par = b % 2
            b3 = ppt[:, 512 * par: 512 * par + 260]\
                .rearrange("p (s j) -> p s j", j=65)
            wait("vector", "sMM", e_mm[b])
            if b >= 1:
                emit_div(b - 1)
            op("vector", lambda e, i=b3[:, :, 64]:
               e.tensor_scalar(RECTMP.ap(), i, ATT_EPS, None, op0=ALU.add))
            E["vector"].append(lambda e: e.drain())
            rslot = RECIPB.ap()[:, 4 * (b % 2):4 * (b % 2 + 1)]
            op("vector", lambda e, o=rslot: e.reciprocal(o, RECTMP.ap()))
            E["vector"].append(lambda e: e.drain())

        def emit_div(b):
            par = b % 2
            b3 = ppt[:, 512 * par: 512 * par + 260]\
                .rearrange("p (s j) -> p s j", j=65)
            R = b // 4
            if R >= 2:
                wait("vector", "sMM", e_tb[R - 2])
            rslot = RECIPB.ap()[:, 4 * (b % 2):4 * (b % 2 + 1)]
            stg = STG.ap()[:, 1024 * (R % 2) + 256 * (b % 4):
                           1024 * (R % 2) + 256 * (b % 4 + 1)]
            e_div[b] = op_fenced(
                "vector",
                lambda e, o=stg.rearrange("p (s j) -> p s j", j=64),
                i=b3[:, :, 0:64],
                r=rslot.unsqueeze(2).broadcast_to([128, 4, 64]):
                e.tensor_tensor(o, i, r, op=ALU.mult), "sEV")

        def emit_pt_transposes(b):
            R = b // 4
            bank = ptr[:, 1024 * (R % 2):1024 * (R % 2 + 1)]
            wait("tensor", "sEV", e_div[b])
            if b % 4 == 0:
                if R >= 2:
                    wait("tensor", "sSE", e_ev2_v[R - 2])
                elif c > 0 and R == 0:
                    pass
            stg = STG.ap()[:, 1024 * (R % 2) + 256 * (b % 4):
                           1024 * (R % 2) + 256 * (b % 4 + 1)]
            tdrain()
            for m in range(2):
                last = (b % 4 == 3 and m == 1)
                op("tensor",
                   lambda e, o=bank[:, 256 * (b % 4) + 128 * m:
                                    256 * (b % 4) + 128 * (m + 1)],
                   i=stg[:, 128 * m:128 * (m + 1)]:
                   e.transpose(o, i, IDB.ap()))
                if last:
                    C["sMM"] += 1
                    e_tb[R] = C["sMM"]
                    E["tensor"].append(
                        lambda e, s=sems["sMM"]: e.drain().then_inc(s, 1))

        def emit_ev2(R):
            bank = ptr[:, 1024 * (R % 2):1024 * (R % 2 + 1)]
            src = [bank[0:64, :].rearrange("p (m u i) -> p m u i", u=2, i=64)
                   .transpose([0, 3, 1, 2]),
                   bank[64:128, :].rearrange("p (m u i) -> p m u i", u=2, i=64)
                   .transpose([0, 3, 1, 2])]
            dst = [obf[0:64, :].rearrange("p (i t) -> p i t", t=HALF)
                   [:, :, 16 * R:16 * (R + 1)]
                   .rearrange("p i (m u) -> p i m u", u=2),
                   obf[64:128, :].rearrange("p (i t) -> p i t", t=HALF)
                   [:, :, 16 * R:16 * (R + 1)]
                   .rearrange("p i (m u) -> p i m u", u=2)]
            wait("scalar", "sMM", e_tb[R])
            E["scalar"].append(lambda e, o=dst[0], i=src[0]: e.copy(o, i))
            e_ev2_s[R] = op_fenced(
                "scalar", lambda e, o=dst[1], i=src[1]: e.copy(o, i), "sSE")
            e_ev2_v[R] = e_ev2_s[R]

        for b in range(NB):
            emit_pt_mms(b)
            emit_pt_round(b)
            if b >= 1:
                emit_pt_transposes(b - 1)
            if b >= 1 and (b - 1) % 4 == 3:
                emit_ev2((b - 1) // 4)
        emit_div(NB - 1)
        emit_pt_transposes(NB - 1)
        emit_ev2(NTB - 1)
        pt_end_T = e_tb[NTB - 1]

        e_wo = [None] * 8
        e_woev = [None] * 8
        slab_arr = [None] * 8

        def emit_slab_dma(ct):
            slab = WOSLAB[ct % 2]
            dq = "gpsimd" if chunk0 else "sync"
            wait(dq, "sMM", pt_end_T if ct < 2 else e_wo[ct - 2])
            if chunk0:
                t_cast = dma("gpsimd",
                             slab[0:64, :].rearrange("p (i c) -> p i c", c=128),
                             wo3[:, :, 128 * ct:128 * (ct + 1)], "sWg")
                wait("gpsimd", "sWg", t_cast)
                t_dup = dma("gpsimd", slab[64:128, :], slab[0:64, :], "sWg")
                slab_arr[ct] = ("sWg", t_dup)
                wait("gpsimd", "sWg", t_dup)
                dma("gpsimd", cwo[ct], slab, "sCW")
            else:
                slab_arr[ct] = ("sWs", dma("sync", slab, cwo[ct], "sWs"))

        emit_slab_dma(0)
        emit_slab_dma(1)
        if c + 1 < NCH:
            emit_xload(c + 1, 0)
            emit_xload(c + 1, 1)
            if _DEBUG and c == _DBG_CHUNK:
                wait("vector", "sY", stats_dump_tgt)
            emit_stats_pair(c + 1, 0, pt_end_T)
        if _DEBUG and c == _DBG_CHUNK:
            wait("sync", "sSE", e_ev2_v[NTB - 1])
            dma("sync", dbg["outb"], obf, "sY")
        wait("tensor", "sSE", e_ev2_v[NTB - 1])
        tdrain()
        for ct in range(8):
            slab = WOSLAB[ct % 2]
            s3 = slab.rearrange("p (i c) -> p i c", c=128)
            wait("tensor", slab_arr[ct][0], slab_arr[ct][1])
            if ct >= 2:
                wait("tensor", "sSE", e_woev[ct - 2])
            bank_h1 = pwo[:, 0:256] if ct % 2 == 0 else ppt[:, 0:256]
            bank_h2 = pwo[:, 512:768] if ct % 2 == 0 else ppt[:, 512:768]
            ev = None
            for i in range(64):
                st = (i == 0)
                sp = (i == 63)
                op("tensor", lambda e, o=bank_h1, l=s3[0:64, i, :],
                   r=OV_I[0][:, i, :], st=st, sp=sp:
                   e.matmul(o, l, r, start=st, stop=sp, skip_group_check=True))
                ev = op("tensor", lambda e, o=bank_h2, l=s3[64:128, i, :],
                        r=OV_I[1][:, i, :], st=st, sp=sp:
                        e.matmul(o, l, r, start=st, stop=sp,
                                 skip_group_check=True),
                        inc=("sMM" if sp else None))
            e_wo[ct] = ev
            wait("scalar", "sMM", ev)
            op("scalar", lambda e, i=bank_h1, o=YFM[:, 512 * ct:512 * ct + 256],
               b=BOv.ap()[:, ct:ct + 1]:
               e.activation(o, i, AF.Identity, bias=b))
            e_woev[ct] = op_fenced(
                "scalar",
                lambda e, i=bank_h2,
                o=YFM[:, 512 * ct + 256:512 * ct + 512],
                b=BOv.ap()[:, ct:ct + 1]:
                e.activation(o, i, AF.Identity, bias=b), "sSE")
            if ct + 2 < 8:
                emit_slab_dma(ct + 2)

        if _DEBUG and c == _DBG_CHUNK:
            wait("sync", "sSE", e_woev[7])
            dma("sync", dbg["xnfm2"], XNFM.ap(), "sY")
        ffn_in = [XNFM.ap(), H1FM, H2FM]
        ffn_cache = [cwf, ca1, ca2]
        ffn_f32 = [wf3, a13, a23]
        e_ffn = [[None] * 8 for _ in range(3)]
        fcw_tgt = {}
        e_fev = [[None] * 8 for _ in range(3)]
        tanh_ev = [None] * 8
        add_ev = [None] * 8
        fstrip_consumed = {}
        fidx = [0]
        for s in range(3):
            for g in range(8):
                fi = fidx[0]
                fidx[0] += 1
                slot = FSTRIP[fi % 3]
                dq = "gpsimd" if chunk0 else "sync"
                if fi < 3:
                    wait(dq, "sMM", pt_end_T)
                else:
                    wait(dq, "sMM", fstrip_consumed[fi - 3])
                    if chunk0 and (fi - 3) in fcw_tgt:
                        wait("gpsimd", "sCW", fcw_tgt[fi - 3])
                if chunk0:
                    dma("gpsimd", slot.rearrange("p (kt c) -> p kt c", c=128),
                        ffn_f32[s][:, :, 128 * g:128 * (g + 1)], "sWg")
                    arr = ("sWg", C["sWg"])
                    wait("gpsimd", "sWg", arr[1])
                    fcw_tgt[fi] = dma("gpsimd", ffn_cache[s][g], slot, "sCW")
                else:
                    arr = ("sWs", dma("sync", slot, ffn_cache[s][g], "sWs"))
                half = pmm[:, 512 * (g % 2):512 * (g % 2 + 1)]
                wait("tensor", arr[0], arr[1])
                if s == 0 and g < 2:
                    sname, tgtv = v_ev[NG - 2 + g]
                    wait("tensor", sname, tgtv)
                if g >= 2 or s > 0:
                    pg = (s, g - 2) if g >= 2 else (s - 1, g + 6)
                    sname = "sSE" if pg[0] == 2 else "sEV"
                    tgtl = e_fev[pg[0]][pg[1]]
                    wait("tensor", sname, tgtl)
                if s > 0 and g == 0:
                    wait("tensor", "sEV", e_fev[s - 1][7])
                mmev = None
                for kt in range(NKT):
                    mmev = op("tensor",
                              lambda e, o=half,
                              l=slot.rearrange("p (kt c) -> p kt c", c=128)[:, kt, :],
                              r=ffn_in[s][:, TC * kt:TC * (kt + 1)],
                              st=(kt == 0), sp=(kt == NKT - 1):
                              e.matmul(o, l, r, start=st, stop=sp),
                              inc=("sMM" if kt == NKT - 1 else None))
                fstrip_consumed[fi] = mmev
                e_ffn[s][g] = mmev
                if s < 2:
                    dst = (H1FM if s == 0 else H2FM)
                    wait("vector", "sMM", mmev)
                    e_fev[s][g] = op_fenced(
                        "vector",
                        lambda e, i=half, o=dst[:, TC * g:TC * (g + 1)]:
                        e.tensor_copy(o, i), "sEV")
                else:
                    wait("scalar", "sMM", mmev)
                    if g >= 2:
                        wait("scalar", "sEV", add_ev[g - 2])
                    h3 = H3TMP[g % 2]
                    tanh_ev[g] = op("scalar",
                                    lambda e, i=half, o=h3, b=FBv.ap()[:, g:g + 1]:
                                    e.activation(o, i, AF.Tanh, bias=b), inc="sSE")
                    e_fev[s][g] = tanh_ev[g]
                    wait("vector", "sSE", tanh_ev[g])
                    add_ev[g] = op_fenced(
                        "vector",
                        lambda e, a=YFM[:, TC * g:TC * (g + 1)], h=h3:
                        e.tensor_tensor(a, a, h, op=ALU.add), "sEV")

        if _DEBUG and c == _DBG_CHUNK:
            wait("sync", "sEV", add_ev[7])
            dma("sync", dbg["yfm"], YFM, "sY")

        yev_list = []
        y_dma_tgts = {}
        xr_tgts = {}
        ev_prev_tile = None

        def emit_xr(tt):
            if tt >= 2:
                wait("sync", "sEV", yev_list[2 * (tt - 2) + 1])
            xr_tgts[tt] = dma(
                "sync", XR.ap()[:, 1024 * (tt % 2):1024 * (tt % 2 + 1)],
                dx[tok0 + 128 * tt: tok0 + 128 * (tt + 1), :], "sX")

        emit_xr(0)
        emit_xr(1)
        for tt in range(NTT):
            ytm = YTM[tt % 2]
            yevs = []
            for hb in range(2):
                bank = pmm[:, 512 * hb:512 * (hb + 1)]
                if tt == 0 and hb == 0:
                    wait("tensor", "sSE", tanh_ev[7])
                    wait("tensor", "sEV", add_ev[7])
                if tt >= 1:
                    wait("tensor", "sEV", ev_prev_tile)
                tev = None
                if hb == 0:
                    tdrain()
                for kq in range(4):
                    kt = 4 * hb + kq
                    tev = op("tensor",
                             lambda e, o=bank[:, 128 * kq:128 * (kq + 1)],
                             i=YFM[:, TC * kt + 128 * tt:TC * kt + 128 * (tt + 1)]:
                             e.transpose(o, i, IDF.ap()),
                             inc=("sMM" if kq == 3 else None))
                wait("vector", "sMM", tev)
                wait("vector", "sX", xr_tgts[tt])
                if tt >= 2:
                    wait("vector", "sY", y_dma_tgts[tt - 2])
                yev = op("vector",
                         lambda e, o=ytm[:, 512 * hb:512 * (hb + 1)], i=bank,
                         x=XR.ap()[:, 1024 * (tt % 2) + 512 * hb:
                                   1024 * (tt % 2) + 512 * (hb + 1)]:
                         e.scalar_tensor_tensor(o, i, 0.0, x,
                                                op0=ALU.bypass, op1=ALU.add),
                         inc="sEV")
                yevs.append(yev)
            ev_prev_tile = yevs[-1]
            yev_list.extend(yevs)
            wait("sync", "sEV", yevs[-1])
            y_dma_tgts[tt] = dma(
                "sync", dy[tok0 + 128 * tt: tok0 + 128 * (tt + 1), :], ytm, "sY")
            if tt + 2 < NTT:
                emit_xr(tt + 2)
        ev_yev.append(yev_list)
        if c + 1 < NCH:
            if os.environ.get("NO_EARLY_STATS") == "1":
                emit_stats_pair(c + 1, 0, pt_end_T)
            emit_xn(c + 1, 0)
            emit_xn(c + 1, 1)
            emit_xload(c + 1, 2)
            emit_xload(c + 1, 3)
            emit_stats_pair(c + 1, 1, None)
            emit_xn(c + 1, 2)
            emit_xn(c + 1, 3)
        last_phase_evacs[0] = [("sEV", yev_list[-4]), ("sEV", yev_list[-1])]
        last_T_event[0] = C["sMM"]
        last_S_event[0] = C["sSE"]
        if chunk0:
            cache_done = C["sCW"]
            wait("sync", "sCW", cache_done)

    if _DEBUG:
        for gg in range(NG):
            dma("sync", dbg["cv"][gg], cv[gg], "sY")
    wait("sync", "sY", C["sY"])

    with nc.Block() as block:
        @block.sync
        def _(eng):
            for f in E["sync"]:
                f(eng)

        @block.gpsimd
        def _(eng):
            for f in E["gpsimd"]:
                f(eng)

        @block.tensor
        def _(eng):
            for f in E["tensor"]:
                f(eng)

        @block.vector
        def _(eng):
            for f in E["vector"]:
                f(eng)

        @block.scalar
        def _(eng):
            for f in E["scalar"]:
                f(eng)

    return nc


_NC_CACHE = None


def kernel(L0_attn_x, L0_attn_Wq, L0_attn_Wk, L0_attn_Wv, L0_attn_Wo,
           L0_attn_bo, L0_ffn_weight, L0_ffn_aux_weight1, L0_ffn_aux_weight2,
           L0_ffn_bias):
    global _NC_CACHE
    x = np.ascontiguousarray(np.asarray(L0_attn_x, dtype=np.float32))
    wq = np.ascontiguousarray(np.asarray(L0_attn_Wq, dtype=np.float32))
    wk = np.ascontiguousarray(np.asarray(L0_attn_Wk, dtype=np.float32))
    wv = np.ascontiguousarray(np.asarray(L0_attn_Wv, dtype=np.float32))
    wo = np.ascontiguousarray(np.asarray(L0_attn_Wo, dtype=np.float32))
    bo = np.ascontiguousarray(np.asarray(L0_attn_bo, dtype=np.float32))
    wf = np.ascontiguousarray(np.asarray(L0_ffn_weight, dtype=np.float32))
    a1 = np.ascontiguousarray(np.asarray(L0_ffn_aux_weight1, dtype=np.float32))
    a2 = np.ascontiguousarray(np.asarray(L0_ffn_aux_weight2, dtype=np.float32))
    fb = np.ascontiguousarray(np.asarray(L0_ffn_bias, dtype=np.float32))
    idb = np.eye(128, dtype=np.float32).astype(ml_dtypes.bfloat16)
    idf = np.eye(128, dtype=np.float32)

    if _NC_CACHE is None:
        _NC_CACHE = build_nc()
    nc = _NC_CACHE
    common = {"wq": wq, "wk": wk, "wv": wv, "wo": wo, "bo": bo,
              "wf": wf, "a1": a1, "a2": a2, "fb": fb, "idb": idb, "idf": idf}
    in_maps = [dict(common, x=x[c]) for c in range(NCORES)]
    res = run_bass_kernel_spmd(nc, in_maps, core_ids=list(range(NCORES)))
    if _DEBUG:
        for k in ["dbg_xnfm", "dbg_qbuf", "dbg_vkbuf", "dbg_outb", "dbg_yfm"]:
            np.save("/tmp/%s.npy" % k,
                    np.asarray(res.results[0][k]).astype(np.float32))
        np.save("/tmp/dbg_outb_all.npy",
                np.stack([np.asarray(res.results[c]["dbg_outb"]).astype(np.float32)
                          for c in range(NCORES)]))
        np.save("/tmp/dbg_vkbuf_all.npy",
                np.stack([np.asarray(res.results[c]["dbg_vkbuf"]).astype(np.float32)
                          for c in range(NCORES)]))
        np.save("/tmp/dbg_qbuf_all.npy",
                np.stack([np.asarray(res.results[c]["dbg_qbuf"]).astype(np.float32)
                          for c in range(NCORES)]))
        np.save("/tmp/dbg_xnfm2_all.npy",
                np.stack([np.asarray(res.results[c]["dbg_xnfm2"]).astype(np.float32)
                          for c in range(NCORES)]))
        np.save("/tmp/dbg_cv_all.npy",
                np.stack([np.asarray(res.results[c]["dbg_cv"]).astype(np.float32)
                          for c in range(NCORES)]))
        np.save("/tmp/dbg_kpair_all.npy",
                np.stack([np.asarray(res.results[c]["dbg_kpair"]).astype(np.float32)
                          for c in range(NCORES)]))
        np.save("/tmp/dbg_stats_all.npy",
                np.stack([np.asarray(res.results[c]["dbg_stats"]).astype(np.float32)
                          for c in range(NCORES)]))
        np.save("/tmp/dbg_xnfm_all.npy",
                np.stack([np.asarray(res.results[c]["dbg_xnfm"]).astype(np.float32)
                          for c in range(NCORES)]))
    y = np.stack([np.asarray(res.results[c]["y"]) for c in range(NCORES)])
    return y
